# version 22
# baseline (speedup 1.0000x reference)
import contextlib

import numpy as np

import concourse.bass as bass
import concourse.mybir as mybir
from concourse.bass_utils import run_bass_kernel_spmd

F32 = mybir.dt.float32
BF16 = mybir.dt.bfloat16
I32 = mybir.dt.int32
Alu = mybir.AluOpType
Act = mybir.ActivationFunctionType
AX = mybir.AxisListType

NCORES = 8
SEQ_PER_CORE = 2
S = 2048
D = 1024
NTB = S // 128
NQT = S // 512
NKC = D // 128
C_ATT = float(np.float32(1.0) / np.sqrt(np.float32(128.0)))
LN_EPS = 1e-5

ENGS = ("sync", "tensor", "vector", "scalar", "gpsimd")


class Sem:
    __slots__ = ("name", "n", "h")

    def __init__(self, name):
        self.name = name
        self.n = 0
        self.h = None

    def tok(self):
        return (self, self.n)


class Plan:
    def __init__(self):
        self.streams = {e: [] for e in ENGS}

    def emit(self, eng, build, waits=(), inc=None, amt=1):
        self.streams[eng].append((tuple(waits), build, inc, amt))
        if inc is not None:
            inc.n += amt
        return inc.tok() if inc is not None else None


def build_nc():
    nc = bass.Bass("TRN2", target_bir_lowering=False)

    x_hbm = nc.dram_tensor("x", [SEQ_PER_CORE * S, D], F32, kind="ExternalInput")
    mask_hbm = nc.dram_tensor("mask", [128, 1], I32, kind="ExternalInput")
    wq_hbm = nc.dram_tensor("wq", [D, 128], F32, kind="ExternalInput")
    wk_hbm = nc.dram_tensor("wk", [D, 128], F32, kind="ExternalInput")
    wv_hbm = nc.dram_tensor("wv", [D, 128], F32, kind="ExternalInput")
    wo_hbm = nc.dram_tensor("wo", [128, D], F32, kind="ExternalInput")
    kern_hbm = nc.dram_tensor("kern", [D, D], F32, kind="ExternalInput")
    bias_hbm = nc.dram_tensor("fbias", [1, D], F32, kind="ExternalInput")
    a_hbm = nc.dram_tensor("la", [D, 16], F32, kind="ExternalInput")
    b_hbm = nc.dram_tensor("lb", [16, D], F32, kind="ExternalInput")
    y_hbm = nc.dram_tensor("y", [SEQ_PER_CORE * S, D], F32, kind="ExternalOutput")

    plan = Plan()
    s_pe = Sem("s_pe")
    s_dve = Sem("s_dve")
    s_act = Sem("s_act")
    s_gp = Sem("s_gp")
    s_load = [Sem(f"s_load{i}") for i in range(6)]
    s_kld = [Sem(f"s_kld{i}") for i in range(2)]
    s_st = [Sem(f"s_st{i}") for i in range(3)]
    sems = [s_pe, s_dve, s_act, s_gp] + s_load + s_st + s_kld
    dma_idx = {"load": 0, "store": 0, "kload": 0}

    ctx = contextlib.ExitStack()

    def sb(name, shape, dtype):
        return ctx.enter_context(nc.sbuf_tensor(name, shape, dtype))

    iof = sb("iof", [128, 128], I32)
    iop = sb("iop", [128, 128], I32)
    identf = sb("identf", [128, 128], F32)
    onesr = sb("onesr", [1, 512], F32)
    maskr = sb("maskr", [128, 1], I32)
    maskf = sb("maskf", [128, 1], F32)
    biasr = sb("biasr", [1, D], F32)
    wstpair = sb("wstpair", [128, 2 * D], F32)
    wst = [wstpair[:, 0:D], wstpair[:, D:2 * D]]
    wq_h = sb("wq_h", [128, D], BF16); wq_l = sb("wq_l", [128, D], BF16)
    wk_h = sb("wk_h", [128, D], BF16); wk_l = sb("wk_l", [128, D], BF16)
    wv_h = sb("wv_h", [128, D], BF16); wv_l = sb("wv_l", [128, D], BF16)
    wo_h = sb("wo_h", [128, D], BF16); wo_l = sb("wo_l", [128, D], BF16)
    q_sb = sb("q_sb", [128, NKC * D], BF16)
    a_st = sb("a_st", [128, 128], F32)
    a_h = sb("a_h", [128, 128], BF16); a_l = sb("a_l", [128, 128], BF16)
    bs_h = sb("bs_h", [16, D], BF16); bs_l = sb("bs_l", [16, D], BF16)
    ones2 = sb("ones2", [2, 512], BF16)
    bias2 = sb("bias2", [2, D], BF16)
    km16 = sb("km16", [128, 16], F32)
    kmaxc = sb("kmaxc", [128, 1], F32)
    krow = sb("krow", [1, 128], F32)
    ksc = sb("ksc", [1, 2], F32)
    invsc = sb("invsc", [128, 2], F32)
    kti = sb("kti", [128, D], I32)
    m1b = sb("m1b", [128, 1], F32)
    ssum = sb("ssum", [128, 2 * NTB], F32)
    ssq = sb("ssq", [128, 2 * NTB], F32)
    smean = sb("smean", [128, 2 * NTB], F32)
    svar = sb("svar", [128, 2 * NTB], F32)
    srstd = sb("srstd", [128, 2 * NTB], F32)
    stmp = sb("stmp", [128, 4], F32)
    snmr = sb("snmr", [128, 2 * NTB], F32)
    rsums = sb("rsums", [128, 2 * NTB], F32)
    nmax = sb("nmax", [128, 1], F32)
    nm4 = sb("nm4", [128, 4], F32)
    asum = sb("asum", [128, 1], F32)
    asum4 = sb("asum4", [128, 4], F32)
    epsb = sb("epsb", [128, 1], F32)
    catt = sb("catt", [1, 2], F32)
    cattb = sb("cattb", [128, 1], F32)
    xq = sb("xq", [128, 4 * D], F32)
    xbuf = [xq[:, i * D:(i + 1) * D] for i in range(4)]
    xn_tm = [sb(f"xn_tm{i}", [128, D], F32) for i in range(2)]
    xfh = [sb(f"xfh{i}", [128, NKC * 512], BF16) for i in range(2)]
    xfl = [sb(f"xfl{i}", [128, NKC * 512], BF16) for i in range(2)]
    Qh = sb("Qh", [128, S], BF16); Ql = sb("Ql", [128, S], BF16)
    Kh = sb("Kh", [128, S], BF16); Kl = sb("Kl", [128, S], BF16)
    Vf = sb("Vf", [128, 512], F32)
    Vt = sb("Vt", [128, S], F32)
    A_sb = [sb(f"A_sb{i}", [128, S], F32) for i in range(2)]
    ATf = sb("ATf", [128, 16 * 256], F32)
    hTh = sb("hTh", [128, S], BF16); hTl = sb("hTl", [128, S], BF16)
    xa_h = sb("xa_h", [16, S], BF16); xa_l = sb("xa_l", [16, S], BF16)
    g_sb = sb("g_sb", [128, NKC * 512], F32)
    xrel = wst
    outb = xbuf
    A1_addr = nc.lookup_mloc(A_sb[1]).addr
    xn_tm.append(nc.alloc_sbuf_tensor_at("xn_tm2", [128, D], F32,
                                         offset=A1_addr))
    xn_tm.append(nc.alloc_sbuf_tensor_at("xn_tm3", [128, D], F32,
                                         offset=A1_addr + 4096))

    g_addr = nc.lookup_mloc(g_sb).addr
    kq_y = nc.alloc_sbuf_tensor_at("kq_y", [128, D], F32, offset=g_addr)
    ktf = nc.alloc_sbuf_tensor_at("ktf", [128, D], F32, offset=g_addr + 4096)
    kdd = nc.alloc_sbuf_tensor_at("kdd", [128, D], F32, offset=g_addr + 8192)
    b_st = nc.alloc_sbuf_tensor_at("b_st", [16, D], F32, offset=g_addr + 12288)
    blrow = nc.alloc_sbuf_tensor_at("blrow", [1, D], BF16,
                                    offset=g_addr + 14336)

    PF = ctx.enter_context(nc.psum_tensor("PF", [128, 1024], F32))
    PA = ctx.enter_context(nc.psum_tensor("PA", [128, 1024], F32))
    PS = ctx.enter_context(nc.psum_tensor("PS", [128, 2048], F32))

    ESEM = {"vector": s_dve, "gpsimd": s_gp, "scalar": s_act, "tensor": s_pe}

    def mk_mm(out, lhsT, rhs, start, stop, is_t=False):
        return (lambda e, o=out, l=lhsT, r=rhs, st=start, sp=stop, t=is_t:
                e.matmul(o, l, r, start=st, stop=sp, is_transpose=t))

    def mm_group(specs, waits, ):
        tok = None
        n = len(specs)
        for i, sp in enumerate(specs):
            is_t = sp[5] if len(sp) > 5 else False
            tok = plan.emit(
                "tensor", mk_mm(sp[0], sp[1], sp[2], sp[3], sp[4], is_t),
                waits=waits if i == 0 else (),
                inc=s_pe if i == n - 1 else None)
        return tok

    def ts1(eng, out, in0, sc, op, waits=()):
        return plan.emit(
            eng, lambda e, o=out, i=in0, s=sc, p=op: e.tensor_scalar(o, i, s, None, p),
            waits, ESEM[eng])

    def ts2(eng, out, in0, s1, s2, op1, op2, waits=()):
        return plan.emit(
            eng,
            lambda e, o=out, i=in0, a=s1, b=s2, p=op1, q=op2: e.tensor_scalar(o, i, a, b, p, q),
            waits, ESEM[eng])

    def stt(eng, out, in0, sc, in1, op0, op1, waits=()):
        return plan.emit(
            eng,
            lambda e, o=out, i=in0, s=sc, j=in1, p=op0, q=op1: e.scalar_tensor_tensor(o, i, s, j, p, q),
            waits, ESEM[eng])

    def tt(eng, out, in0, in1, op, waits=()):
        return plan.emit(
            eng, lambda e, o=out, i=in0, j=in1, p=op: e.tensor_tensor(o, i, j, p),
            waits, ESEM[eng])

    def cp(eng, out, in0, waits=()):
        if eng == "scalar":
            return plan.emit(eng, lambda e, o=out, i=in0: e.copy(o, i),
                             waits, s_act)
        return plan.emit(eng, lambda e, o=out, i=in0: e.tensor_copy(o, i),
                         waits, ESEM[eng])

    def act(out, in0, func, bias=0.0, scale=1.0, accum=None, waits=()):
        return plan.emit(
            "scalar",
            (lambda e, o=out, i=in0, f=func, b=bias, s=scale, a=accum:
             e.activation(o, i, f, bias=b, scale=s, accum_out=a)),
            waits, s_act)

    def scopy(out, in0, scale=1.0, waits=()):
        return act(out, in0, Act.Copy, bias=0.0, scale=scale, waits=waits)

    def _dma(kind, eng, pool, dst, src, waits):
        sem = pool[dma_idx[kind] % len(pool)]
        dma_idx[kind] += 1
        w = list(waits)
        if sem.n > 0:
            w.append((sem, sem.n))
        return plan.emit(eng, lambda e, d=dst, s=src: e.dma_start(d, s),
                         w, sem, 16)

    def load(dst, src, waits=()):
        return _dma("load", "sync", s_load, dst, src, waits)

    def kload(dst, src, waits=()):
        return _dma("kload", "gpsimd", s_kld, dst, src, waits)

    def store(dst, src, waits=()):
        return _dma("store", "scalar", s_st, dst, src, waits)

    def vred(out, in0, op, negate=False, absval=False, waits=()):
        def b(e, o=out, i=in0, p=op, n=negate, a=absval):
            return e.tensor_reduce(o, i, AX.X, p, apply_absolute_value=a,
                                   negate=n)
        return plan.emit("vector", b, waits, s_dve)

    def vrecip(out, in0, waits=()):
        return plan.emit("vector", lambda e, o=out, i=in0: e.reciprocal(o, i),
                         waits, s_dve)

    def memset(eng, ap, v, waits=()):
        return plan.emit(eng, lambda e, a=ap, x=v: e.memset(a, x),
                         waits, ESEM[eng])

    def W(*toks):
        return [t for t in toks if t is not None]

    t_iota = plan.emit(
        "gpsimd",
        lambda e: (e.iota(iof[:, :], [[1, 128]], channel_multiplier=0),
                   e.iota(iop[:, :], [[0, 128]], channel_multiplier=1))[-1],
        (), s_gp)
    t_identf = tt("vector", identf[:, :], iof[:, :], iop[:, :], Alu.is_equal,
                  waits=[t_iota])
    t_ones = memset("vector", onesr[:, :], 1.0)
    t_eps = memset("vector", epsb[:, :], LN_EPS)
    t_one2 = memset("vector", ones2[:, :], 1.0)
    t_c0 = memset("vector", catt[0:1, 0:1], 128.0)
    t_c1 = act(catt[0:1, 1:2], catt[0:1, 0:1], Act.Sqrt, waits=[t_c0])
    t_c2 = vrecip(catt[0:1, 0:1], catt[0:1, 1:2], waits=[t_c1])

    SLOT = (2, 3, 0, 1)
    pre_x = [load(xbuf[SLOT[tb4]], x_hbm[tb4 * 128:(tb4 + 1) * 128, :])
             for tb4 in range(4)]

    t_ml = load(maskr[:, :], mask_hbm[:, :])
    t_maskf = cp("vector", maskf[:, :], maskr[:, :], waits=[t_ml])

    t_bl = load(biasr[:, :], bias_hbm[:, :])
    t_al = load(a_st.rearrange("p (c r) -> p c r", c=NKC),
                a_hbm.rearrange("(c p) r -> p c r", p=128))
    t_bload = load(b_st[:, :], b_hbm[:, :])
    t_ah = cp("vector", a_h[:, :], a_st[:, :], waits=[t_al])
    t_alo = tt("vector", a_l[:, :], a_st[:, :], a_h[:, :], Alu.subtract,
               waits=[t_ah])

    wsplit_last = None
    w_free = [None, None]
    for i, (hbm, hi, lo, is_wo) in enumerate([
            (wq_hbm, wq_h, wq_l, False), (wk_hbm, wk_h, wk_l, False),
            (wv_hbm, wv_h, wv_l, False), (wo_hbm, wo_h, wo_l, True)]):
        stg = wst[i % 2]
        w = W(w_free[i % 2])
        if is_wo:
            t_ld = load(stg, hbm[:, :], waits=w)
            t_m = ts1("vector", stg, stg, maskf[:, 0:1], Alu.mult,
                      waits=[t_ld, t_maskf])
            t_hi = cp("gpsimd", hi[:, :], stg, waits=[t_m])
        else:
            t_ld = load(stg.rearrange("p (c n) -> p c n", c=NKC),
                        hbm.rearrange("(c p) n -> p c n", p=128), waits=w)
            t_hi = cp("gpsimd", hi[:, :], stg, waits=[t_ld])
        t_lo = tt("gpsimd", lo[:, :], stg, hi[:, :], Alu.subtract,
                  waits=[t_hi])
        w_free[i % 2] = t_lo
        wsplit_last = t_lo

    t_cb = mm_group([(PA[:, 0:1], onesr[0:1, 0:128], catt[0:1, 0:1], True,
                      True)], waits=[t_c2, t_ones, t_identf])
    t_cbe = cp("vector", cattb[:, :], PA[:, 0:1], waits=[t_cb])

    t_m1b = memset("vector", m1b[:, :], -1.0)

    kst_free = [w_free[0], w_free[1]]
    qtoks = {}

    def quant_stage0():
        last_r = None
        for c in range(NKC):
            stg = wst[c % 2]
            t_ld = kload(stg, kern_hbm[c * 128:(c + 1) * 128, :],
                         waits=W(kst_free[c % 2]))
            t_r = vred(km16[:, c:c + 1], stg, Alu.max, absval=True,
                       waits=[t_ld])
            kst_free[c % 2] = t_r
            last_r = t_r
        qtoks["absmax"] = last_r

    def quant_stage1(pa0_guard):
        t_km = vred(kmaxc[:, :], km16[:, 0:NKC], Alu.max,
                    waits=[qtoks["absmax"]])
        t_tp0 = mm_group([(PA[0:1, 0:128], kmaxc[:, :], identf[:, :], True,
                           True, True)], waits=W(t_km, pa0_guard))
        t_krow = cp("vector", krow[:, :], PA[0:1, 0:128], waits=[t_tp0])
        t_kM = vred(ksc[0:1, 0:1], krow[:, :], Alu.max, waits=[t_krow])
        t_i = vrecip(stmp[0:1, 0:1], ksc[0:1, 0:1], waits=[t_kM])
        t_inv = ts1("vector", ksc[0:1, 0:1], stmp[0:1, 0:1], 7.0, Alu.mult,
                    waits=[t_i])
        t_sc = vrecip(ksc[0:1, 1:2], ksc[0:1, 0:1], waits=[t_inv])
        t_bc = mm_group([(PA[:, 0:2], onesr[0:1, 0:128], ksc[0:1, 0:2], True,
                          True)], waits=[t_sc, t_krow])
        t_is = cp("vector", invsc[:, :], PA[:, 0:2], waits=[t_bc])
        t_be = ts1("vector", biasr[:, :], biasr[:, :], invsc[0:1, 0:1],
                   Alu.mult, waits=[t_bl, t_is])
        t_bsh = ts1("vector", bs_h[:, :], b_st[:, :], invsc[0:16, 0:1],
                    Alu.mult, waits=[t_bload, t_is])
        t_bsl = stt("vector", bs_l[:, :], b_st[:, :], invsc[0:16, 0:1],
                    bs_h[:, :], Alu.mult, Alu.subtract, waits=[t_bsh])
        t_bh = cp("vector", bias2[0:1, :], biasr[:, :], waits=[t_be, t_bsl])
        t_blo = tt("vector", blrow[:, :], biasr[:, :], bias2[0:1, :],
                   Alu.subtract, waits=[t_bh])
        t_d2 = load(bias2[1:2, :], blrow[:, :], waits=[t_blo])
        qtoks.update(t_bsh=t_bsh, t_bsl=t_bsl, t_be=t_be, t_is=t_is,
                     t_bh=t_bh, t_d2=t_d2)
        return t_is

    def quant_chunks(cs):
        t_is = qtoks["t_is"]
        for c in cs:
            stg = wst[c % 2]
            t_ld = kload(stg, kern_hbm[c * 128:(c + 1) * 128, :],
                         waits=W(kst_free[c % 2]))
            qdst = q_sb[:, c * D:(c + 1) * D]
            u1 = plan.emit("scalar",
                           lambda e, o=kq_y[:, :], i=stg:
                           e.mul(o, i, invsc[:, 0:1]),
                           W(t_ld, t_is, qtoks.get("qlast")), s_act)
            u2 = cp("scalar", kti[:, :], kq_y[:, :], waits=[u1])
            u3 = ts1("vector", ktf[:, :], kti[:, :], 1.0, Alu.subtract,
                     waits=[u2])
            u4 = stt("vector", kdd[:, :], ktf[:, :], 1.0, kq_y[:, :],
                     Alu.add, Alu.subtract, waits=[u3])
            u5 = stt("vector", kq_y[:, :], kdd[:, :], -0.5, ktf[:, :],
                     Alu.is_le, Alu.add, waits=[u4])
            u6 = stt("vector", qdst, kdd[:, :], 0.5, kq_y[:, :],
                     Alu.is_le, Alu.add, waits=[u5])
            kst_free[c % 2] = u6
            qtoks["qlast"] = u6
            qtoks["t_qdone"] = u6

    xbuf_free = [None, None, None, None]
    xntm_free = [None, None, None, None]
    xf_free = [(None, None), (None, None)]
    pf_free = [t_identf, t_identf]
    ab_ps = [t_identf, t_identf]
    pa_free = [t_cbe, None]
    state = {
        "ps_free": None, "A_free": None, "AT_free": None,
        "Vt_ready": None, "QK_ready": None, "hT_ready": None,
        "g_free": None,
    }
    xrel_free = [None, None]
    outb_free = [None, None]
    total_stores = [0]

    def xn_pre(tb, src, src_tok, eng, stats_wait, i=None, stb=None):
        if i is None:
            i = tb % 2
        if stb is None:
            stb = tb
        t_xn = ts2("vector", xn_tm[i][:, :], src,
                   smean[:, stb:stb + 1], srstd[:, stb:stb + 1],
                   Alu.subtract, Alu.mult,
                   waits=W(src_tok, stats_wait, xntm_free[i]))
        return t_xn

    def xn_tps(tb, jbuf, t_xn, psd, ps_guard, i=None):
        if not isinstance(ps_guard, list):
            ps_guard = [ps_guard]
        """psd: [128, 1024] fp32 psum view (2 banks). Returns
        (t_eh, t_el)."""
        if i is None:
            i = tb % 2
        tb4 = tb % 4
        sp = [(psd[:, c * 128:(c + 1) * 128],
               xn_tm[i][:, c * 128:(c + 1) * 128], identf[:, :],
               c % 4 == 0, c % 4 == 3, True) for c in range(8)]
        t_tp = mm_group(sp, waits=W(t_xn, *ps_guard))
        xntm_free[i] = t_tp
        dsth = xfh[jbuf].rearrange("p (c t) -> p c t", t=512)[:, :, tb4 * 128:(tb4 + 1) * 128]
        dstl = xfl[jbuf].rearrange("p (c t) -> p c t", t=512)[:, :, tb4 * 128:(tb4 + 1) * 128]
        srcv = psd.rearrange("p (c t) -> p c t", t=128)
        t_eh = cp("scalar", dsth, srcv, waits=W(t_tp, xf_free[jbuf][0]))
        t_el = plan.emit(
            "vector",
            (lambda e, o=dstl, ii=srcv, hh=dsth:
             e.tensor_tensor(o, ii, hh, Alu.subtract)),
            W(t_eh, xf_free[jbuf][1]), s_dve)
        return t_eh, t_el

    def seq_program(s, quant=False):
        row0 = s * S
        sb0 = s * NTB

        stats_tok = [None] * NQT
        ln_toks = [[None] * 4 for _ in range(NQT)]

        def emit_ln_loads_stats(qt):
            if s == 0 and qt == 0:
                sq_toks = list(pre_x)
            else:
                sq_toks = []
                for tb4 in range(4):
                    r = row0 + (qt * 4 + tb4) * 128
                    sq_toks.append(load(xbuf[SLOT[tb4]], x_hbm[r:r + 128, :],
                                        waits=W(xbuf_free[SLOT[tb4]])))
            sums = []
            for tb4 in range(4):
                tb = qt * 4 + tb4
                sl = SLOT[tb4]
                t_sum = vred(ssum[:, sb0 + tb:sb0 + tb + 1], xbuf[sl], Alu.add,
                             waits=[sq_toks[tb4]])
                t_sq = act(xn_tm[sl][:, :], xbuf[sl], Act.Square,
                           accum=ssq[:, sb0 + tb:sb0 + tb + 1],
                           waits=W(sq_toks[tb4], xntm_free[sl]))
                xntm_free[sl] = t_sq
                sums.append((t_sum, t_sq))
            c0 = qt * 4
            t1 = ts1("vector", smean[:, sb0 + c0:sb0 + c0 + 4], ssum[:, sb0 + c0:sb0 + c0 + 4],
                     1.0 / D, Alu.mult, waits=[sums[-1][0], sums[-1][1]])
            t2 = ts1("vector", svar[:, sb0 + c0:sb0 + c0 + 4], ssq[:, sb0 + c0:sb0 + c0 + 4],
                     1.0 / D, Alu.mult, waits=[t1])
            t3 = tt("vector", stmp[:, 0:4], smean[:, sb0 + c0:sb0 + c0 + 4],
                    smean[:, sb0 + c0:sb0 + c0 + 4], Alu.mult, waits=[t2])
            t4 = tt("vector", svar[:, sb0 + c0:sb0 + c0 + 4], svar[:, sb0 + c0:sb0 + c0 + 4],
                    stmp[:, 0:4], Alu.subtract, waits=[t3])
            t5 = act(svar[:, sb0 + c0:sb0 + c0 + 4], svar[:, sb0 + c0:sb0 + c0 + 4], Act.Ln,
                     bias=epsb[:, 0:1], waits=[t4, t_eps])
            t6 = act(srstd[:, sb0 + c0:sb0 + c0 + 4], svar[:, sb0 + c0:sb0 + c0 + 4], Act.Exp,
                     scale=-0.5, waits=[t5])
            t7 = stt("vector", snmr[:, sb0 + c0:sb0 + c0 + 4], smean[:, sb0 + c0:sb0 + c0 + 4], -1.0,
                     srstd[:, sb0 + c0:sb0 + c0 + 4], Alu.mult, Alu.mult, waits=[t6])
            stats_tok[qt] = t7

        def emit_ln_xn(qt, half):
            for tb4 in (2 * half, 2 * half + 1):
                tb = qt * 4 + tb4
                sl = SLOT[tb4]
                ln_toks[qt][tb4] = xn_pre(tb, xbuf[sl], None, "vector",
                                          stats_tok[qt], i=sl,
                                          stb=sb0 + tb)
                xbuf_free[sl] = ln_toks[qt][tb4]

        def emit_ln_tps(qt, half):
            j = qt % 2
            for tb4 in (2 * half, 2 * half + 1):
                tb = qt * 4 + tb4
                hp = tb4 % 2
                ln_toks[qt][tb4] = xn_tps(
                    tb, j, ln_toks[qt][tb4],
                    PS[:, hp * 1024:(hp + 1) * 1024], ab_ps[hp],
                    i=SLOT[tb4])
                ab_ps[hp] = ln_toks[qt][tb4][1]

        def proj(qt, wh, wl, half, extra):
            j = qt % 2
            ehs = [p[0] for p in ln_toks[qt]]
            els = [p[1] for p in ln_toks[qt]]
            ps = PA[:, half * 512:(half + 1) * 512]
            for c in range(8):
                plan.emit("tensor",
                          mk_mm(ps, wh[:, c * 128:(c + 1) * 128],
                                xfh[j][:, c * 512:(c + 1) * 512],
                                c == 0, False),
                          waits=W(*ehs, *extra, wsplit_last) if c == 0 else ())
            for c in range(8):
                plan.emit("tensor",
                          mk_mm(ps, wl[:, c * 128:(c + 1) * 128],
                                xfh[j][:, c * 512:(c + 1) * 512],
                                False, False))
            tok = None
            for c in range(8):
                tok = plan.emit("tensor",
                                mk_mm(ps, wh[:, c * 128:(c + 1) * 128],
                                      xfl[j][:, c * 512:(c + 1) * 512],
                                      False, c == 7),
                                waits=W(*els) if c == 0 else (),
                                inc=s_pe if c == 7 else None)
            return tok

        def emit_q(qt):
            qcols = slice(qt * 512, (qt + 1) * 512)
            t_q = proj(qt, wq_h, wq_l, 0, W(pa_free[0]))
            t_qh = scopy(Qh[:, qcols], PA[:, 0:512], scale=cattb[:, 0:1],
                         waits=[t_q, t_cbe])
            t_ql = stt("vector", Ql[:, qcols], PA[:, 0:512], cattb[:, 0:1],
                       Qh[:, qcols], Alu.mult, Alu.subtract,
                       waits=[t_qh, t_cbe])
            pa_free[0] = t_ql
            if qt == NQT - 1:
                state["Q_ready"] = t_ql

        def emit_k(qt):
            qcols = slice(qt * 512, (qt + 1) * 512)
            t_k = proj(qt, wk_h, wk_l, 1, W(pa_free[1]))
            t_kh = scopy(Kh[:, qcols], PA[:, 512:1024], waits=[t_k])
            t_kl = tt("vector", Kl[:, qcols], PA[:, 512:1024], Kh[:, qcols],
                      Alu.subtract, waits=[t_kh])
            pa_free[1] = t_kl
            if qt == NQT - 1:
                state["QK_ready"] = (state["Q_ready"], t_kl)

        vtoks = {}

        def emit_v(qt):
            j = qt % 2
            t_v = proj(qt, wv_h, wv_l, 0, W(pa_free[0]))
            t_vf = cp("vector", Vf[:, :], PA[:, 0:512], waits=[t_v])
            pa_free[0] = t_vf
            xf_free[j] = (t_v, t_v)
            vtoks[qt] = t_vf

        def emit_vtp(qt):
            qcols = slice(qt * 512, (qt + 1) * 512)
            t_vf = vtoks[qt]
            sp = [(PF[:, tb4 * 128:(tb4 + 1) * 128],
                   Vf[:, tb4 * 128:(tb4 + 1) * 128], identf[:, :],
                   tb4 % 4 == 0, tb4 % 4 == 3, True) for tb4 in range(4)]
            t_tv = mm_group(sp, waits=W(t_vf, pf_free[0]))
            t_ev = scopy(Vt[:, qcols], PF[:, 0:512], waits=[t_tv])
            pf_free[0] = t_ev
            if qt == NQT - 1:
                state["Vt_ready"] = (t_ev, t_ev)

        for it in range(NQT + 1):
            if it < NQT:
                emit_ln_loads_stats(it)
            if it > 0:
                emit_q(it - 1)
            if it < NQT:
                emit_ln_xn(it, 0)
            if it > 0:
                emit_k(it - 1)
            if it < NQT:
                emit_ln_xn(it, 1)
            if it > 0:
                emit_v(it - 1)
            if it < NQT:
                emit_ln_tps(it, 0)
                emit_ln_tps(it, 1)
            if it > 0:
                emit_vtp(it - 1)
            if quant:
                if it == 1:
                    quant_stage0()
                elif it == 2:
                    pa_free[0] = quant_stage1(pa_free[0])
                    quant_chunks([0])
                elif it == 3:
                    quant_chunks([1])
                elif it == 4:
                    quant_chunks([2])

        A_free = state.get("A_free2", [None, None])
        AT_free = state["AT_free"]
        hT_last = None
        exp_toks = [None] * 16
        exp_bk = [[None] * 4 for _ in range(16)]

        def a_transposes(qb):
            nonlocal AT_free
            qh2 = qb % 2
            ab = A_sb[qb % 2]
            last_l = None
            tgts = [(PA, 0, pa_free), (PA, 1, pa_free),
                    (PF, 0, pf_free), (PF, 1, pf_free)]
            for rr in range(4):
                pt, half, freelist = tgts[rr]
                sp = [(pt[:, half * 512 + kk * 128: half * 512 + (kk + 1) * 128],
                       ab[:, (4 * rr + kk) * 128:(4 * rr + kk + 1) * 128],
                       identf[:, :], kk == 0, kk == 3, True)
                      for kk in range(4)]
                t_tp = mm_group(sp, waits=W(exp_bk[qb][rr], freelist[half],
                                            AT_free))
                dsth = ATf.rearrange("p (kb q) -> p kb q", q=256)[:, 4 * rr:4 * rr + 4, qh2 * 128:(qh2 + 1) * 128]
                srcv = pt.rearrange("p (h kb q) -> p h kb q", h=2, q=128)[:, half, :, :]
                t_h = cp("scalar", dsth, srcv, waits=[t_tp])
                freelist[half] = t_h
                last_l = t_h
            A_free[qb % 2] = last_l

        def av_pair(pair):
            nonlocal AT_free, hT_last
            quarter = pair % 2
            po = quarter * 512
            specs = []
            for kb in range(16):
                vsl = slice(kb * 128, (kb + 1) * 128)
                asl = slice(kb * 256, (kb + 1) * 256)
                specs.append((PA[:, po:po + 256], Vt[:, vsl],
                              ATf[:, asl], kb == 0, kb == 15))
            t_av = mm_group(specs,
                            waits=W(A_free[1], state["Vt_ready"][0],
                                    state["Vt_ready"][1], pa_free[quarter]))
            AT_free = t_av
            hsl = slice(pair * 256, (pair + 1) * 256)
            t_hh = scopy(hTh[:, hsl], PA[:, po:po + 256], waits=[t_av])
            t_hl = tt("vector", hTl[:, hsl], PA[:, po:po + 256],
                      hTh[:, hsl], Alu.subtract, waits=[t_hh])
            pa_free[quarter] = t_hl
            hT_last = t_hl

        ps_bank = [ab_ps[0], ab_ps[0], ab_ps[1], ab_ps[1]]
        for qb in range(16):
            qsl = slice(qb * 128, (qb + 1) * 128)
            pm_last = None
            for kb4 in range(4):
                ks = slice(kb4 * 512, (kb4 + 1) * 512)
                plan.emit("tensor", mk_mm(PS[:, ks], Qh[:, qsl], Kh[:, ks],
                                          True, False),
                          waits=W(state["QK_ready"][0], state["QK_ready"][1],
                                  ps_bank[kb4]))
                plan.emit("tensor", mk_mm(PS[:, ks], Qh[:, qsl], Kl[:, ks],
                                          False, False))
                t_b = plan.emit("tensor", mk_mm(PS[:, ks], Ql[:, qsl],
                                                Kh[:, ks], False, True),
                                inc=s_pe)
                pm_last = vred(nm4[:, kb4:kb4 + 1], PS[:, ks], Alu.max,
                               waits=[t_b])
            t_nm = vred(nmax[:, :], nm4[:, 0:4], Alu.max, negate=True,
                        waits=[pm_last])
            t_exp = None
            for kb4 in range(4):
                ks = slice(kb4 * 512, (kb4 + 1) * 512)
                t_exp = act(A_sb[qb % 2][:, ks], PS[:, ks], Act.Exp,
                            bias=nmax[:, 0:1], scale=1.0,
                            accum=asum4[:, kb4:kb4 + 1],
                            waits=W(t_nm, A_free[qb % 2] if kb4 == 0 else None))
                ps_bank[kb4] = t_exp
                exp_bk[qb][kb4] = t_exp
            exp_toks[qb] = t_exp
            t_as = vred(asum[:, :], asum4[:, 0:4], Alu.add, waits=[t_exp])
            vrecip(rsums[:, sb0 + qb:sb0 + qb + 1], asum[:, 0:1], waits=[t_as])
            if qb >= 3 and qb % 2 == 1:
                av_pair((qb - 3) // 2)
            if qb >= 1:
                a_transposes(qb - 1)
            if quant and qb in (2, 5, 8, 11, 14):
                quant_chunks([3 + (2, 5, 8, 11, 14).index(qb)])
                if qb == 14:
                    xrel_free[0] = kst_free[0]
                    xrel_free[1] = kst_free[1]
        a_transposes(15)
        av_pair(7)
        state["ps_free"] = ps_bank[3]
        state["A_free2"] = A_free
        state["AT_free"] = AT_free
        state["hT_ready"] = hT_last

        gparts = [[None] * 4 for _ in range(NQT)]
        gxa = [None] * NQT
        ps_half = [state["ps_free"], state["ps_free"]]

        def emit_gln_pre(qt, half):
            for tb4 in (2 * half, 2 * half + 1):
                tb = qt * 4 + tb4
                i = tb % 2
                r = row0 + tb * 128
                t_x = load(xrel[i], x_hbm[r:r + 128, :],
                           waits=W(xrel_free[i]))
                gparts[qt][tb4] = xn_pre(tb, xrel[i], t_x, "vector",
                                         stats_tok[qt], stb=sb0 + tb)
                xrel_free[i] = gparts[qt][tb4]

        def emit_gln_tps(qt, half):
            j = qt % 2
            for tb4 in (2 * half, 2 * half + 1):
                tb = qt * 4 + tb4
                gparts[qt][tb4] = xn_tps(tb, j, gparts[qt][tb4],
                                         PF[:, :],
                                         W(pf_free[0], pf_free[1]))
                pf_free[0] = gparts[qt][tb4][1]
                pf_free[1] = gparts[qt][tb4][1]

        def emit_xa(qt):
            j = qt % 2
            qcols = slice(qt * 512, (qt + 1) * 512)
            evs = [t for pair in gparts[qt] for t in pair]
            specs = []
            for c in range(8):
                specs.append((PA[0:16, 512:1024], a_h[:, c * 16:(c + 1) * 16],
                              xfh[j][:, c * 512:(c + 1) * 512], c == 0, False))
                specs.append((PA[0:16, 512:1024], a_h[:, c * 16:(c + 1) * 16],
                              xfl[j][:, c * 512:(c + 1) * 512], False, False))
            for c in range(8):
                specs.append((PA[0:16, 512:1024], a_l[:, c * 16:(c + 1) * 16],
                              xfh[j][:, c * 512:(c + 1) * 512], False, c == 7))
            t_xa = mm_group(specs, waits=W(*evs, pa_free[1], t_ah, t_alo))
            t_xah = scopy(xa_h[:, qcols], PA[0:16, 512:1024], waits=[t_xa])
            t_xal = tt("vector", xa_l[:, qcols], PA[0:16, 512:1024],
                       xa_h[:, qcols], Alu.subtract, waits=[t_xah])
            pa_free[1] = t_xal
            gxa[qt] = (t_xah, t_xal)

        def emit_ffn_chunks(qt, cs):
            j = qt % 2
            qcols = slice(qt * 512, (qt + 1) * 512)
            t_xah, t_xal = gxa[qt]
            for c in cs:
                half = c % 2
                csl = slice(c * 512, (c + 1) * 512)
                ps = PA[:, half * 512:(half + 1) * 512]
                specs = []
                for kc in range(8):
                    st = q_sb[:, kc * D + c * 128: kc * D + (c + 1) * 128]
                    specs.append((ps, st,
                                  xfh[j][:, kc * 512:(kc + 1) * 512],
                                  kc == 0, False))
                    specs.append((ps, st,
                                  xfl[j][:, kc * 512:(kc + 1) * 512],
                                  False, False))
                bsl = slice(c * 128, (c + 1) * 128)
                specs.append((ps, bs_h[:, bsl], xa_h[:, qcols], False, False))
                specs.append((ps, bs_h[:, bsl], xa_l[:, qcols], False, False))
                specs.append((ps, bs_l[:, bsl], xa_h[:, qcols], False, False))
                specs.append((ps, bias2[:, bsl], ones2[:, 0:512],
                              False, True))
                t_y = mm_group(specs, waits=W(pa_free[half], t_xah, t_xal,
                                              qtoks["t_qdone"],
                                              qtoks["t_bsh"], qtoks["t_bsl"],
                                              qtoks["t_bh"], qtoks["t_d2"],
                                              t_one2))
                t_g = act(g_sb[:, csl], ps, Act.Silu, scale=invsc[:, 1:2],
                          waits=W(t_y, state["g_free"]))
                pa_free[half] = t_g
                gparts[qt][0 if c < 4 else 1] = t_y
                gsilu[qt][c] = t_g

        def emit_wo_sg(qt):
            qcols = slice(qt * 512, (qt + 1) * 512)
            for c in range(8):
                half = c % 2
                csl = slice(c * 512, (c + 1) * 512)
                wsl = slice(c * 128, (c + 1) * 128)
                ps = PA[:, half * 512:(half + 1) * 512]
                specs = [
                    (ps, wo_h[:, wsl], hTh[:, qcols], True, False),
                    (ps, wo_h[:, wsl], hTl[:, qcols], False, False),
                    (ps, wo_l[:, wsl], hTh[:, qcols], False, True),
                ]
                t_s = mm_group(specs, waits=W(pa_free[half],
                                              state["hT_ready"]))
                t_sg = tt("vector", g_sb[:, csl], ps, g_sb[:, csl], Alu.mult,
                          waits=W(t_s, gsilu[qt][c]))
                pa_free[half] = t_sg
                gsilu[qt][c] = t_sg

        def emit_sgtp_stores(qt):
            j = qt % 2
            last_tp = None
            for tb4 in range(4):
                tb = qt * 4 + tb4
                i = tb % 2
                hp = tb4 % 2
                r = row0 + tb * 128
                t_xr = load(xrel[i], x_hbm[r:r + 128, :],
                            waits=W(xrel_free[i]))
                sp = [(PS[:, hp * 1024 + c * 128: hp * 1024 + (c + 1) * 128],
                       g_sb[:, c * 512 + tb4 * 128: c * 512 + tb4 * 128 + 128],
                       identf[:, :], c % 4 == 0, c % 4 == 3, True)
                      for c in range(8)]
                t_tp = mm_group(sp, waits=W(gsilu[qt][7], ps_half[hp]))
                last_tp = t_tp
                t_out = stt("vector", outb[i],
                            PS[:, hp * 1024:(hp + 1) * 1024],
                            rsums[:, sb0 + tb:sb0 + tb + 1], xrel[i],
                            Alu.mult, Alu.add,
                            waits=W(t_tp, t_xr, outb_free[i],
                                    xbuf_free[i]))
                ps_half[hp] = t_out
                xrel_free[i] = t_out
                t_st = store(y_hbm[r:r + 128, :], outb[i],
                             waits=[t_out])
                outb_free[i] = t_st
                xbuf_free[i] = t_st
                total_stores[0] += 1
            state["g_free"] = last_tp
            xf_free[j] = (gparts[qt][0], gparts[qt][1])

        gsilu = [[None] * 8 for _ in range(NQT)]
        for it in range(NQT + 1):
            if it < NQT:
                emit_gln_pre(it, 0)
            if it > 0:
                emit_ffn_chunks(it - 1, [0, 1, 2, 3])
            if it < NQT:
                emit_gln_tps(it, 0)
                emit_gln_pre(it, 1)
            if it > 0:
                emit_ffn_chunks(it - 1, [4, 5, 6, 7])
            if it < NQT:
                emit_gln_tps(it, 1)
            if it > 0:
                emit_wo_sg(it - 1)
            if it < NQT:
                emit_xa(it)
            if it > 0:
                emit_sgtp_stores(it - 1)
        state["ps_free"] = ps_half[1]

    for s in range(SEQ_PER_CORE):
        seq_program(s, quant=(s == 0))

    def final_wait(e):
        last = None
        for sem in s_st:
            if sem.n > 0:
                last = e.wait_ge(sem.h, sem.n)
        return last
    plan.emit("scalar", final_wait, (), None)

    with ctx:
        for sem in sems:
            sem.h = ctx.enter_context(nc.semaphore(sem.name))

        def runner(stream):
            def run(e):
                last_wait = {}
                for waits, build, inc, amt in stream:
                    for sem, v in waits:
                        if v is None or v <= 0:
                            continue
                        if last_wait.get(sem.name, -1) >= v:
                            continue
                        e.wait_ge(sem.h, v)
                        last_wait[sem.name] = v
                    inst = build(e)
                    if inc is not None:
                        inst.then_inc(inc.h, amt)
            return run

        with nc.Block() as block:
            block.sync(runner(plan.streams["sync"]))
            block.tensor(runner(plan.streams["tensor"]))
            block.vector(runner(plan.streams["vector"]))
            block.scalar(runner(plan.streams["scalar"]))
            block.gpsimd(runner(plan.streams["gpsimd"]))
    return nc


_NC_CACHE = None


def kernel(L0_attn_x, L0_attn_head_dim_mask, L0_attn_w_q, L0_attn_w_k,
           L0_attn_w_v, L0_attn_w_o, L0_ffn_kernel, L0_ffn_bias, L0_ffn_a,
           L0_ffn_b):
    global _NC_CACHE
    x = np.ascontiguousarray(np.asarray(L0_attn_x, dtype=np.float32))
    mask = np.ascontiguousarray(
        np.asarray(L0_attn_head_dim_mask, dtype=np.int32).reshape(128, 1))
    wq = np.ascontiguousarray(np.asarray(L0_attn_w_q, dtype=np.float32))
    wk = np.ascontiguousarray(np.asarray(L0_attn_w_k, dtype=np.float32))
    wv = np.ascontiguousarray(np.asarray(L0_attn_w_v, dtype=np.float32))
    wo = np.ascontiguousarray(np.asarray(L0_attn_w_o, dtype=np.float32))
    kern = np.ascontiguousarray(np.asarray(L0_ffn_kernel, dtype=np.float32))
    fbias = np.ascontiguousarray(
        np.asarray(L0_ffn_bias, dtype=np.float32).reshape(1, D))
    la = np.ascontiguousarray(np.asarray(L0_ffn_a, dtype=np.float32))
    lb = np.ascontiguousarray(np.asarray(L0_ffn_b, dtype=np.float32))

    if _NC_CACHE is None:
        _NC_CACHE = build_nc()
    nc = _NC_CACHE

    in_maps = []
    for c in range(NCORES):
        xs = np.ascontiguousarray(
            x[c * SEQ_PER_CORE:(c + 1) * SEQ_PER_CORE].reshape(
                SEQ_PER_CORE * S, D))
        in_maps.append({
            "x": xs, "mask": mask, "wq": wq, "wk": wk, "wv": wv, "wo": wo,
            "kern": kern, "fbias": fbias, "la": la, "lb": lb,
        })
    res = run_bass_kernel_spmd(nc, in_maps, core_ids=list(range(NCORES)))
    out = np.concatenate([np.asarray(r["y"]).reshape(SEQ_PER_CORE, S, D)
                          for r in res.results], axis=0)
    return out.astype(np.float32)


# version 23 (final)
# speedup vs baseline: 1.0095x; 1.0095x over previous
import contextlib

import numpy as np

import concourse.bass as bass
import concourse.mybir as mybir
from concourse.bass_utils import run_bass_kernel_spmd

F32 = mybir.dt.float32
BF16 = mybir.dt.bfloat16
I32 = mybir.dt.int32
Alu = mybir.AluOpType
Act = mybir.ActivationFunctionType
AX = mybir.AxisListType

NCORES = 8
SEQ_PER_CORE = 2
S = 2048
D = 1024
NTB = S // 128
NQT = S // 512
NKC = D // 128
C_ATT = float(np.float32(1.0) / np.sqrt(np.float32(128.0)))
LN_EPS = 1e-5

ENGS = ("sync", "tensor", "vector", "scalar", "gpsimd")


class Sem:
    __slots__ = ("name", "n", "h")

    def __init__(self, name):
        self.name = name
        self.n = 0
        self.h = None

    def tok(self):
        return (self, self.n)


class Plan:
    def __init__(self):
        self.streams = {e: [] for e in ENGS}

    def emit(self, eng, build, waits=(), inc=None, amt=1):
        self.streams[eng].append((tuple(waits), build, inc, amt))
        if inc is not None:
            inc.n += amt
        return inc.tok() if inc is not None else None


def build_nc():
    nc = bass.Bass("TRN2", target_bir_lowering=False)

    x_hbm = nc.dram_tensor("x", [SEQ_PER_CORE * S, D], F32, kind="ExternalInput")
    mask_hbm = nc.dram_tensor("mask", [128, 1], I32, kind="ExternalInput")
    wq_hbm = nc.dram_tensor("wq", [D, 128], F32, kind="ExternalInput")
    wk_hbm = nc.dram_tensor("wk", [D, 128], F32, kind="ExternalInput")
    wv_hbm = nc.dram_tensor("wv", [D, 128], F32, kind="ExternalInput")
    wo_hbm = nc.dram_tensor("wo", [128, D], F32, kind="ExternalInput")
    kern_hbm = nc.dram_tensor("kern", [D, D], F32, kind="ExternalInput")
    bias_hbm = nc.dram_tensor("fbias", [1, D], F32, kind="ExternalInput")
    a_hbm = nc.dram_tensor("la", [D, 16], F32, kind="ExternalInput")
    b_hbm = nc.dram_tensor("lb", [16, D], F32, kind="ExternalInput")
    y_hbm = nc.dram_tensor("y", [SEQ_PER_CORE * S, D], F32, kind="ExternalOutput")

    plan = Plan()
    s_pe = Sem("s_pe")
    s_dve = Sem("s_dve")
    s_act = Sem("s_act")
    s_gp = Sem("s_gp")
    s_load = [Sem(f"s_load{i}") for i in range(6)]
    s_kld = [Sem(f"s_kld{i}") for i in range(2)]
    s_st = [Sem(f"s_st{i}") for i in range(3)]
    sems = [s_pe, s_dve, s_act, s_gp] + s_load + s_st + s_kld
    dma_idx = {"load": 0, "store": 0, "kload": 0}

    ctx = contextlib.ExitStack()

    def sb(name, shape, dtype):
        return ctx.enter_context(nc.sbuf_tensor(name, shape, dtype))

    iof = sb("iof", [128, 128], I32)
    iop = sb("iop", [128, 128], I32)
    identf = sb("identf", [128, 128], F32)
    onesr = sb("onesr", [1, 512], F32)
    maskr = sb("maskr", [128, 1], I32)
    maskf = sb("maskf", [128, 1], F32)
    biasr = sb("biasr", [1, D], F32)
    wstpair = sb("wstpair", [128, 2 * D], F32)
    wst = [wstpair[:, 0:D], wstpair[:, D:2 * D]]
    wq_h = sb("wq_h", [128, D], BF16); wq_l = sb("wq_l", [128, D], BF16)
    wk_h = sb("wk_h", [128, D], BF16); wk_l = sb("wk_l", [128, D], BF16)
    wv_h = sb("wv_h", [128, D], BF16); wv_l = sb("wv_l", [128, D], BF16)
    wo_h = sb("wo_h", [128, D], BF16); wo_l = sb("wo_l", [128, D], BF16)
    q_sb = sb("q_sb", [128, NKC * D], BF16)
    a_st = sb("a_st", [128, 128], F32)
    a_h = sb("a_h", [128, 128], BF16); a_l = sb("a_l", [128, 128], BF16)
    bs_h = sb("bs_h", [16, D], BF16); bs_l = sb("bs_l", [16, D], BF16)
    ones2 = sb("ones2", [2, 512], BF16)
    bias2 = sb("bias2", [2, D], BF16)
    km16 = sb("km16", [128, 16], F32)
    kmaxc = sb("kmaxc", [128, 1], F32)
    krow = sb("krow", [1, 128], F32)
    ksc = sb("ksc", [1, 2], F32)
    invsc = sb("invsc", [128, 2], F32)
    kti = sb("kti", [128, D], I32)
    m1b = sb("m1b", [128, 1], F32)
    ssum = sb("ssum", [128, 2 * NTB], F32)
    ssq = sb("ssq", [128, 2 * NTB], F32)
    smean = sb("smean", [128, 2 * NTB], F32)
    svar = sb("svar", [128, 2 * NTB], F32)
    srstd = sb("srstd", [128, 2 * NTB], F32)
    stmp = sb("stmp", [128, 4], F32)
    snmr = sb("snmr", [128, 2 * NTB], F32)
    rsums = sb("rsums", [128, 2 * NTB], F32)
    nmax = sb("nmax", [128, 1], F32)
    nm4 = sb("nm4", [128, 4], F32)
    asum = sb("asum", [128, 1], F32)
    asum4 = sb("asum4", [128, 4], F32)
    epsb = sb("epsb", [128, 1], F32)
    catt = sb("catt", [1, 2], F32)
    cattb = sb("cattb", [128, 1], F32)
    xq = sb("xq", [128, 4 * D], F32)
    xbuf = [xq[:, i * D:(i + 1) * D] for i in range(4)]
    xn_tm = [sb(f"xn_tm{i}", [128, D], F32) for i in range(2)]
    xfh = [sb(f"xfh{i}", [128, NKC * 512], BF16) for i in range(2)]
    xfl = [sb(f"xfl{i}", [128, NKC * 512], BF16) for i in range(2)]
    Qh = sb("Qh", [128, S], BF16); Ql = sb("Ql", [128, S], BF16)
    Kh = sb("Kh", [128, S], BF16); Kl = sb("Kl", [128, S], BF16)
    Vf = sb("Vf", [128, 512], F32)
    Vt = sb("Vt", [128, S], F32)
    A_sb = [sb(f"A_sb{i}", [128, S], F32) for i in range(2)]
    ATf = sb("ATf", [128, 16 * 256], F32)
    hTh = sb("hTh", [128, S], BF16); hTl = sb("hTl", [128, S], BF16)
    xa_h = sb("xa_h", [16, S], BF16); xa_l = sb("xa_l", [16, S], BF16)
    g_sb = sb("g_sb", [128, NKC * 512], F32)
    xrel = wst
    outb = xbuf
    A1_addr = nc.lookup_mloc(A_sb[1]).addr
    xn_tm.append(nc.alloc_sbuf_tensor_at("xn_tm2", [128, D], F32,
                                         offset=A1_addr))
    xn_tm.append(nc.alloc_sbuf_tensor_at("xn_tm3", [128, D], F32,
                                         offset=A1_addr + 4096))

    g_addr = nc.lookup_mloc(g_sb).addr
    kq_y = nc.alloc_sbuf_tensor_at("kq_y", [128, D], F32, offset=g_addr)
    ktf = nc.alloc_sbuf_tensor_at("ktf", [128, D], F32, offset=g_addr + 4096)
    kdd = nc.alloc_sbuf_tensor_at("kdd", [128, D], F32, offset=g_addr + 8192)
    b_st = nc.alloc_sbuf_tensor_at("b_st", [16, D], F32, offset=g_addr + 12288)
    blrow = nc.alloc_sbuf_tensor_at("blrow", [1, D], BF16,
                                    offset=g_addr + 14336)

    PF = ctx.enter_context(nc.psum_tensor("PF", [128, 1024], F32))
    PA = ctx.enter_context(nc.psum_tensor("PA", [128, 1024], F32))
    PS = ctx.enter_context(nc.psum_tensor("PS", [128, 2048], F32))

    ESEM = {"vector": s_dve, "gpsimd": s_gp, "scalar": s_act, "tensor": s_pe}

    def mk_mm(out, lhsT, rhs, start, stop, is_t=False):
        return (lambda e, o=out, l=lhsT, r=rhs, st=start, sp=stop, t=is_t:
                e.matmul(o, l, r, start=st, stop=sp, is_transpose=t))

    def mm_group(specs, waits, ):
        tok = None
        n = len(specs)
        for i, sp in enumerate(specs):
            is_t = sp[5] if len(sp) > 5 else False
            tok = plan.emit(
                "tensor", mk_mm(sp[0], sp[1], sp[2], sp[3], sp[4], is_t),
                waits=waits if i == 0 else (),
                inc=s_pe if i == n - 1 else None)
        return tok

    def ts1(eng, out, in0, sc, op, waits=()):
        return plan.emit(
            eng, lambda e, o=out, i=in0, s=sc, p=op: e.tensor_scalar(o, i, s, None, p),
            waits, ESEM[eng])

    def ts2(eng, out, in0, s1, s2, op1, op2, waits=()):
        return plan.emit(
            eng,
            lambda e, o=out, i=in0, a=s1, b=s2, p=op1, q=op2: e.tensor_scalar(o, i, a, b, p, q),
            waits, ESEM[eng])

    def stt(eng, out, in0, sc, in1, op0, op1, waits=()):
        return plan.emit(
            eng,
            lambda e, o=out, i=in0, s=sc, j=in1, p=op0, q=op1: e.scalar_tensor_tensor(o, i, s, j, p, q),
            waits, ESEM[eng])

    def tt(eng, out, in0, in1, op, waits=()):
        return plan.emit(
            eng, lambda e, o=out, i=in0, j=in1, p=op: e.tensor_tensor(o, i, j, p),
            waits, ESEM[eng])

    def cp(eng, out, in0, waits=()):
        if eng == "scalar":
            return plan.emit(eng, lambda e, o=out, i=in0: e.copy(o, i),
                             waits, s_act)
        return plan.emit(eng, lambda e, o=out, i=in0: e.tensor_copy(o, i),
                         waits, ESEM[eng])

    def act(out, in0, func, bias=0.0, scale=1.0, accum=None, waits=()):
        return plan.emit(
            "scalar",
            (lambda e, o=out, i=in0, f=func, b=bias, s=scale, a=accum:
             e.activation(o, i, f, bias=b, scale=s, accum_out=a)),
            waits, s_act)

    def scopy(out, in0, scale=1.0, waits=()):
        return act(out, in0, Act.Copy, bias=0.0, scale=scale, waits=waits)

    def _dma(kind, eng, pool, dst, src, waits):
        sem = pool[dma_idx[kind] % len(pool)]
        dma_idx[kind] += 1
        w = list(waits)
        if sem.n > 0:
            w.append((sem, sem.n))
        return plan.emit(eng, lambda e, d=dst, s=src: e.dma_start(d, s),
                         w, sem, 16)

    def load(dst, src, waits=()):
        return _dma("load", "sync", s_load, dst, src, waits)

    def kload(dst, src, waits=()):
        return _dma("kload", "gpsimd", s_kld, dst, src, waits)

    def store(dst, src, waits=()):
        return _dma("store", "scalar", s_st, dst, src, waits)

    def vred(out, in0, op, negate=False, absval=False, waits=()):
        def b(e, o=out, i=in0, p=op, n=negate, a=absval):
            return e.tensor_reduce(o, i, AX.X, p, apply_absolute_value=a,
                                   negate=n)
        return plan.emit("vector", b, waits, s_dve)

    def vrecip(out, in0, waits=()):
        return plan.emit("vector", lambda e, o=out, i=in0: e.reciprocal(o, i),
                         waits, s_dve)

    def memset(eng, ap, v, waits=()):
        return plan.emit(eng, lambda e, a=ap, x=v: e.memset(a, x),
                         waits, ESEM[eng])

    def W(*toks):
        return [t for t in toks if t is not None]

    t_iota = plan.emit(
        "gpsimd",
        lambda e: (e.iota(iof[:, :], [[1, 128]], channel_multiplier=0),
                   e.iota(iop[:, :], [[0, 128]], channel_multiplier=1))[-1],
        (), s_gp)
    t_identf = tt("vector", identf[:, :], iof[:, :], iop[:, :], Alu.is_equal,
                  waits=[t_iota])
    t_ones = memset("vector", onesr[:, :], 1.0)
    t_eps = memset("vector", epsb[:, :], LN_EPS)
    t_one2 = memset("vector", ones2[:, :], 1.0)
    t_c0 = memset("vector", catt[0:1, 0:1], 128.0)
    t_c1 = act(catt[0:1, 1:2], catt[0:1, 0:1], Act.Sqrt, waits=[t_c0])
    t_c2 = vrecip(catt[0:1, 0:1], catt[0:1, 1:2], waits=[t_c1])

    SLOT = (2, 3, 0, 1)
    pre_x = [load(xbuf[SLOT[tb4]], x_hbm[tb4 * 128:(tb4 + 1) * 128, :])
             for tb4 in range(4)]

    t_ml = load(maskr[:, :], mask_hbm[:, :])
    t_maskf = cp("vector", maskf[:, :], maskr[:, :], waits=[t_ml])

    t_bl = load(biasr[:, :], bias_hbm[:, :])
    t_al = load(a_st.rearrange("p (c r) -> p c r", c=NKC),
                a_hbm.rearrange("(c p) r -> p c r", p=128))
    t_bload = load(b_st[:, :], b_hbm[:, :])
    t_ah = cp("vector", a_h[:, :], a_st[:, :], waits=[t_al])
    t_alo = tt("vector", a_l[:, :], a_st[:, :], a_h[:, :], Alu.subtract,
               waits=[t_ah])

    wsp = []
    w_free = [None, None]
    for i, (hbm, hi, lo, is_wo) in enumerate([
            (wq_hbm, wq_h, wq_l, False), (wk_hbm, wk_h, wk_l, False),
            (wv_hbm, wv_h, wv_l, False), (wo_hbm, wo_h, wo_l, True)]):
        stg = wst[i % 2]
        w = W(w_free[i % 2])
        if is_wo:
            t_ld = load(stg, hbm[:, :], waits=w)
            t_m = ts1("vector", stg, stg, maskf[:, 0:1], Alu.mult,
                      waits=[t_ld, t_maskf])
            t_hi = cp("gpsimd", hi[:, :], stg, waits=[t_m])
        else:
            t_ld = load(stg.rearrange("p (c n) -> p c n", c=NKC),
                        hbm.rearrange("(c p) n -> p c n", p=128), waits=w)
            t_hi = cp("gpsimd", hi[:, :], stg, waits=[t_ld])
        t_lo = tt("gpsimd", lo[:, :], stg, hi[:, :], Alu.subtract,
                  waits=[t_hi])
        w_free[i % 2] = t_lo
        wsp.append(t_lo)

    t_cb = mm_group([(PA[:, 0:1], onesr[0:1, 0:128], catt[0:1, 0:1], True,
                      True)], waits=[t_c2, t_ones, t_identf])
    t_cbe = cp("vector", cattb[:, :], PA[:, 0:1], waits=[t_cb])

    t_m1b = memset("vector", m1b[:, :], -1.0)

    kst_free = [w_free[0], w_free[1]]
    qtoks = {}

    def quant_stage0():
        last_r = None
        for c in range(NKC):
            stg = wst[c % 2]
            t_ld = kload(stg, kern_hbm[c * 128:(c + 1) * 128, :],
                         waits=W(kst_free[c % 2]))
            t_r = vred(km16[:, c:c + 1], stg, Alu.max, absval=True,
                       waits=[t_ld])
            kst_free[c % 2] = t_r
            last_r = t_r
        qtoks["absmax"] = last_r

    def quant_stage1(pa0_guard):
        t_km = vred(kmaxc[:, :], km16[:, 0:NKC], Alu.max,
                    waits=[qtoks["absmax"]])
        t_tp0 = mm_group([(PA[0:1, 0:128], kmaxc[:, :], identf[:, :], True,
                           True, True)], waits=W(t_km, pa0_guard))
        t_krow = cp("vector", krow[:, :], PA[0:1, 0:128], waits=[t_tp0])
        t_kM = vred(ksc[0:1, 0:1], krow[:, :], Alu.max, waits=[t_krow])
        t_i = vrecip(stmp[0:1, 0:1], ksc[0:1, 0:1], waits=[t_kM])
        t_inv = ts1("vector", ksc[0:1, 0:1], stmp[0:1, 0:1], 7.0, Alu.mult,
                    waits=[t_i])
        t_sc = vrecip(ksc[0:1, 1:2], ksc[0:1, 0:1], waits=[t_inv])
        t_bc = mm_group([(PA[:, 0:2], onesr[0:1, 0:128], ksc[0:1, 0:2], True,
                          True)], waits=[t_sc, t_krow])
        t_is = cp("vector", invsc[:, :], PA[:, 0:2], waits=[t_bc])
        t_be = ts1("vector", biasr[:, :], biasr[:, :], invsc[0:1, 0:1],
                   Alu.mult, waits=[t_bl, t_is])
        t_bsh = ts1("vector", bs_h[:, :], b_st[:, :], invsc[0:16, 0:1],
                    Alu.mult, waits=[t_bload, t_is])
        t_bsl = stt("vector", bs_l[:, :], b_st[:, :], invsc[0:16, 0:1],
                    bs_h[:, :], Alu.mult, Alu.subtract, waits=[t_bsh])
        t_bh = cp("vector", bias2[0:1, :], biasr[:, :], waits=[t_be, t_bsl])
        t_blo = tt("vector", blrow[:, :], biasr[:, :], bias2[0:1, :],
                   Alu.subtract, waits=[t_bh])
        t_d2 = load(bias2[1:2, :], blrow[:, :], waits=[t_blo])
        qtoks.update(t_bsh=t_bsh, t_bsl=t_bsl, t_be=t_be, t_is=t_is,
                     t_bh=t_bh, t_d2=t_d2)
        return t_is

    def quant_chunks(cs):
        t_is = qtoks["t_is"]
        for c in cs:
            stg = wst[c % 2]
            t_ld = kload(stg, kern_hbm[c * 128:(c + 1) * 128, :],
                         waits=W(kst_free[c % 2]))
            qdst = q_sb[:, c * D:(c + 1) * D]
            u1 = plan.emit("scalar",
                           lambda e, o=kq_y[:, :], i=stg:
                           e.mul(o, i, invsc[:, 0:1]),
                           W(t_ld, t_is, qtoks.get("qlast")), s_act)
            u2 = cp("scalar", kti[:, :], kq_y[:, :], waits=[u1])
            u3 = ts1("vector", ktf[:, :], kti[:, :], 1.0, Alu.subtract,
                     waits=[u2])
            u4 = stt("vector", kdd[:, :], ktf[:, :], 1.0, kq_y[:, :],
                     Alu.add, Alu.subtract, waits=[u3])
            u5 = stt("vector", kq_y[:, :], kdd[:, :], -0.5, ktf[:, :],
                     Alu.is_le, Alu.add, waits=[u4])
            u6 = stt("vector", qdst, kdd[:, :], 0.5, kq_y[:, :],
                     Alu.is_le, Alu.add, waits=[u5])
            kst_free[c % 2] = u6
            qtoks["qlast"] = u6
            qtoks["t_qdone"] = u6

    xbuf_free = [None, None, None, None]
    xntm_free = [None, None, None, None]
    xf_free = [(None, None), (None, None)]
    pf_free = [t_identf, t_identf]
    ab_ps = [t_identf, t_identf]
    pa_free = [t_cbe, None]
    state = {
        "ps_free": None, "A_free": None, "AT_free": None,
        "Vt_ready": None, "QK_ready": None, "hT_ready": None,
        "g_free": None,
    }
    xrel_free = [None, None]
    outb_free = [None, None]
    total_stores = [0]

    def xn_pre(tb, src, src_tok, eng, stats_wait, i=None, stb=None):
        if i is None:
            i = tb % 2
        if stb is None:
            stb = tb
        t_xn = ts2("vector", xn_tm[i][:, :], src,
                   smean[:, stb:stb + 1], srstd[:, stb:stb + 1],
                   Alu.subtract, Alu.mult,
                   waits=W(src_tok, stats_wait, xntm_free[i]))
        return t_xn

    def xn_tps(tb, jbuf, t_xn, psd, ps_guard, i=None):
        if not isinstance(ps_guard, list):
            ps_guard = [ps_guard]
        """psd: [128, 1024] fp32 psum view (2 banks). Returns
        (t_eh, t_el)."""
        if i is None:
            i = tb % 2
        tb4 = tb % 4
        sp = [(psd[:, c * 128:(c + 1) * 128],
               xn_tm[i][:, c * 128:(c + 1) * 128], identf[:, :],
               c % 4 == 0, c % 4 == 3, True) for c in range(8)]
        t_tp = mm_group(sp, waits=W(t_xn, *ps_guard))
        xntm_free[i] = t_tp
        dsth = xfh[jbuf].rearrange("p (c t) -> p c t", t=512)[:, :, tb4 * 128:(tb4 + 1) * 128]
        dstl = xfl[jbuf].rearrange("p (c t) -> p c t", t=512)[:, :, tb4 * 128:(tb4 + 1) * 128]
        srcv = psd.rearrange("p (c t) -> p c t", t=128)
        t_eh = cp("scalar", dsth, srcv, waits=W(t_tp, xf_free[jbuf][0]))
        t_el = plan.emit(
            "vector",
            (lambda e, o=dstl, ii=srcv, hh=dsth:
             e.tensor_tensor(o, ii, hh, Alu.subtract)),
            W(t_eh, xf_free[jbuf][1]), s_dve)
        return t_eh, t_el

    def seq_program(s, quant=False):
        row0 = s * S
        sb0 = s * NTB

        stats_tok = [None] * NQT
        ln_toks = [[None] * 4 for _ in range(NQT)]
        pre = state.pop("pre1", None)
        if pre is not None:
            ab_ps[0] = state["ps_free_h"][0]
            ab_ps[1] = state["ps_free_h"][1]

        def emit_ln_loads_stats(qt):
            if qt == 0 and pre is not None:
                stats_tok[0] = pre["stats"]
                return
            if s == 0 and qt == 0:
                sq_toks = list(pre_x)
            else:
                sq_toks = []
                for tb4 in range(4):
                    r = row0 + (qt * 4 + tb4) * 128
                    sq_toks.append(load(xbuf[SLOT[tb4]], x_hbm[r:r + 128, :],
                                        waits=W(xbuf_free[SLOT[tb4]])))
            sums = []
            for tb4 in range(4):
                tb = qt * 4 + tb4
                sl = SLOT[tb4]
                t_sum = vred(ssum[:, sb0 + tb:sb0 + tb + 1], xbuf[sl], Alu.add,
                             waits=[sq_toks[tb4]])
                t_sq = act(xn_tm[sl][:, :], xbuf[sl], Act.Square,
                           accum=ssq[:, sb0 + tb:sb0 + tb + 1],
                           waits=W(sq_toks[tb4], xntm_free[sl]))
                xntm_free[sl] = t_sq
                sums.append((t_sum, t_sq))
            c0 = qt * 4
            t1 = ts1("vector", smean[:, sb0 + c0:sb0 + c0 + 4], ssum[:, sb0 + c0:sb0 + c0 + 4],
                     1.0 / D, Alu.mult, waits=[sums[-1][0], sums[-1][1]])
            t2 = ts1("vector", svar[:, sb0 + c0:sb0 + c0 + 4], ssq[:, sb0 + c0:sb0 + c0 + 4],
                     1.0 / D, Alu.mult, waits=[t1])
            t3 = tt("vector", stmp[:, 0:4], smean[:, sb0 + c0:sb0 + c0 + 4],
                    smean[:, sb0 + c0:sb0 + c0 + 4], Alu.mult, waits=[t2])
            t4 = tt("vector", svar[:, sb0 + c0:sb0 + c0 + 4], svar[:, sb0 + c0:sb0 + c0 + 4],
                    stmp[:, 0:4], Alu.subtract, waits=[t3])
            t5 = act(svar[:, sb0 + c0:sb0 + c0 + 4], svar[:, sb0 + c0:sb0 + c0 + 4], Act.Ln,
                     bias=epsb[:, 0:1], waits=[t4, t_eps])
            t6 = act(srstd[:, sb0 + c0:sb0 + c0 + 4], svar[:, sb0 + c0:sb0 + c0 + 4], Act.Exp,
                     scale=-0.5, waits=[t5])
            t7 = stt("vector", snmr[:, sb0 + c0:sb0 + c0 + 4], smean[:, sb0 + c0:sb0 + c0 + 4], -1.0,
                     srstd[:, sb0 + c0:sb0 + c0 + 4], Alu.mult, Alu.mult, waits=[t6])
            stats_tok[qt] = t7

        def emit_ln_xn(qt, half):
            if qt == 0 and pre is not None:
                if half == 0:
                    ln_toks[0][0] = pre["ln01"][0]
                    ln_toks[0][1] = pre["ln01"][1]
                    xbuf_free[2] = pre["ln01"][0]
                    xbuf_free[3] = pre["ln01"][1]
                else:
                    for tb4 in (2, 3):
                        sl = SLOT[tb4]
                        ln_toks[0][tb4] = xn_pre(
                            tb4, pre["src23"][tb4 - 2],
                            pre["ld23"][tb4 - 2], "vector",
                            stats_tok[0], i=sl, stb=sb0 + tb4)
                return
            for tb4 in (2 * half, 2 * half + 1):
                tb = qt * 4 + tb4
                sl = SLOT[tb4]
                ln_toks[qt][tb4] = xn_pre(tb, xbuf[sl], None, "vector",
                                          stats_tok[qt], i=sl,
                                          stb=sb0 + tb)
                xbuf_free[sl] = ln_toks[qt][tb4]

        def emit_ln_tps(qt, half):
            j = qt % 2
            for tb4 in (2 * half, 2 * half + 1):
                tb = qt * 4 + tb4
                hp = tb4 % 2
                ln_toks[qt][tb4] = xn_tps(
                    tb, j, ln_toks[qt][tb4],
                    PS[:, hp * 1024:(hp + 1) * 1024], ab_ps[hp],
                    i=SLOT[tb4])
                ab_ps[hp] = ln_toks[qt][tb4][1]

        def proj(qt, wh, wl, half, extra, wtok):
            j = qt % 2
            ehs = [p[0] for p in ln_toks[qt]]
            els = [p[1] for p in ln_toks[qt]]
            ps = PA[:, half * 512:(half + 1) * 512]
            for c in range(8):
                plan.emit("tensor",
                          mk_mm(ps, wh[:, c * 128:(c + 1) * 128],
                                xfh[j][:, c * 512:(c + 1) * 512],
                                c == 0, False),
                          waits=W(*ehs, *extra, wtok) if c == 0 else ())
            for c in range(8):
                plan.emit("tensor",
                          mk_mm(ps, wl[:, c * 128:(c + 1) * 128],
                                xfh[j][:, c * 512:(c + 1) * 512],
                                False, False))
            tok = None
            for c in range(8):
                tok = plan.emit("tensor",
                                mk_mm(ps, wh[:, c * 128:(c + 1) * 128],
                                      xfl[j][:, c * 512:(c + 1) * 512],
                                      False, c == 7),
                                waits=W(*els) if c == 0 else (),
                                inc=s_pe if c == 7 else None)
            return tok

        def emit_q(qt):
            qcols = slice(qt * 512, (qt + 1) * 512)
            t_q = proj(qt, wq_h, wq_l, 0, W(pa_free[0]), wsp[0])
            t_qh = scopy(Qh[:, qcols], PA[:, 0:512], scale=cattb[:, 0:1],
                         waits=[t_q, t_cbe])
            t_ql = stt("vector", Ql[:, qcols], PA[:, 0:512], cattb[:, 0:1],
                       Qh[:, qcols], Alu.mult, Alu.subtract,
                       waits=[t_qh, t_cbe])
            pa_free[0] = t_ql
            if qt == NQT - 1:
                state["Q_ready"] = t_ql

        def emit_k(qt):
            qcols = slice(qt * 512, (qt + 1) * 512)
            t_k = proj(qt, wk_h, wk_l, 1, W(pa_free[1]), wsp[1])
            t_kh = scopy(Kh[:, qcols], PA[:, 512:1024], waits=[t_k])
            t_kl = tt("vector", Kl[:, qcols], PA[:, 512:1024], Kh[:, qcols],
                      Alu.subtract, waits=[t_kh])
            pa_free[1] = t_kl
            if qt == NQT - 1:
                state["QK_ready"] = (state["Q_ready"], t_kl)

        vtoks = {}

        def emit_v(qt):
            j = qt % 2
            t_v = proj(qt, wv_h, wv_l, 0, W(pa_free[0]), wsp[2])
            t_vf = cp("vector", Vf[:, :], PA[:, 0:512], waits=[t_v])
            pa_free[0] = t_vf
            xf_free[j] = (t_v, t_v)
            vtoks[qt] = t_vf

        def emit_vtp(qt):
            qcols = slice(qt * 512, (qt + 1) * 512)
            t_vf = vtoks[qt]
            sp = [(PF[:, tb4 * 128:(tb4 + 1) * 128],
                   Vf[:, tb4 * 128:(tb4 + 1) * 128], identf[:, :],
                   tb4 % 4 == 0, tb4 % 4 == 3, True) for tb4 in range(4)]
            t_tv = mm_group(sp, waits=W(t_vf, pf_free[0]))
            t_ev = scopy(Vt[:, qcols], PF[:, 0:512], waits=[t_tv])
            pf_free[0] = t_ev
            if qt == NQT - 1:
                state["Vt_ready"] = (t_ev, t_ev)

        for it in range(NQT + 1):
            if it < NQT:
                emit_ln_loads_stats(it)
            if it > 0:
                emit_q(it - 1)
            if it < NQT:
                emit_ln_xn(it, 0)
            if it > 0:
                emit_k(it - 1)
            if it < NQT:
                emit_ln_xn(it, 1)
            if it > 0:
                emit_v(it - 1)
            if it < NQT:
                emit_ln_tps(it, 0)
                emit_ln_tps(it, 1)
            if it > 0:
                emit_vtp(it - 1)
            if quant:
                if it == 1:
                    quant_stage0()
                elif it == 2:
                    pa_free[0] = quant_stage1(pa_free[0])
                    quant_chunks([0])
                elif it == 3:
                    quant_chunks([1])
                elif it == 4:
                    quant_chunks([2])

        A_free = state.get("A_free2", [None, None])
        AT_free = state["AT_free"]
        hT_last = None
        exp_toks = [None] * 16
        exp_bk = [[None] * 4 for _ in range(16)]

        def a_transposes(qb):
            nonlocal AT_free
            qh2 = qb % 2
            ab = A_sb[qb % 2]
            last_l = None
            tgts = [(PA, 0, pa_free), (PA, 1, pa_free),
                    (PF, 0, pf_free), (PF, 1, pf_free)]
            for rr in range(4):
                pt, half, freelist = tgts[rr]
                sp = [(pt[:, half * 512 + kk * 128: half * 512 + (kk + 1) * 128],
                       ab[:, (4 * rr + kk) * 128:(4 * rr + kk + 1) * 128],
                       identf[:, :], kk == 0, kk == 3, True)
                      for kk in range(4)]
                t_tp = mm_group(sp, waits=W(exp_bk[qb][rr], freelist[half],
                                            AT_free))
                dsth = ATf.rearrange("p (kb q) -> p kb q", q=256)[:, 4 * rr:4 * rr + 4, qh2 * 128:(qh2 + 1) * 128]
                srcv = pt.rearrange("p (h kb q) -> p h kb q", h=2, q=128)[:, half, :, :]
                t_h = cp("scalar", dsth, srcv, waits=[t_tp])
                freelist[half] = t_h
                last_l = t_h
            A_free[qb % 2] = last_l

        def av_pair(pair):
            nonlocal AT_free, hT_last
            quarter = pair % 2
            po = quarter * 512
            specs = []
            for kb in range(16):
                vsl = slice(kb * 128, (kb + 1) * 128)
                asl = slice(kb * 256, (kb + 1) * 256)
                specs.append((PA[:, po:po + 256], Vt[:, vsl],
                              ATf[:, asl], kb == 0, kb == 15))
            t_av = mm_group(specs,
                            waits=W(A_free[1], state["Vt_ready"][0],
                                    state["Vt_ready"][1], pa_free[quarter]))
            AT_free = t_av
            hsl = slice(pair * 256, (pair + 1) * 256)
            t_hh = scopy(hTh[:, hsl], PA[:, po:po + 256], waits=[t_av])
            t_hl = tt("vector", hTl[:, hsl], PA[:, po:po + 256],
                      hTh[:, hsl], Alu.subtract, waits=[t_hh])
            pa_free[quarter] = t_hl
            hT_last = t_hl

        ps_bank = [ab_ps[0], ab_ps[0], ab_ps[1], ab_ps[1]]
        for qb in range(16):
            qsl = slice(qb * 128, (qb + 1) * 128)
            pm_last = None
            for kb4 in range(4):
                ks = slice(kb4 * 512, (kb4 + 1) * 512)
                plan.emit("tensor", mk_mm(PS[:, ks], Qh[:, qsl], Kh[:, ks],
                                          True, False),
                          waits=W(state["QK_ready"][0], state["QK_ready"][1],
                                  ps_bank[kb4]))
                plan.emit("tensor", mk_mm(PS[:, ks], Qh[:, qsl], Kl[:, ks],
                                          False, False))
                t_b = plan.emit("tensor", mk_mm(PS[:, ks], Ql[:, qsl],
                                                Kh[:, ks], False, True),
                                inc=s_pe)
                pm_last = vred(nm4[:, kb4:kb4 + 1], PS[:, ks], Alu.max,
                               waits=[t_b])
            t_nm = vred(nmax[:, :], nm4[:, 0:4], Alu.max, negate=True,
                        waits=[pm_last])
            t_exp = None
            for kb4 in range(4):
                ks = slice(kb4 * 512, (kb4 + 1) * 512)
                t_exp = act(A_sb[qb % 2][:, ks], PS[:, ks], Act.Exp,
                            bias=nmax[:, 0:1], scale=1.0,
                            accum=asum4[:, kb4:kb4 + 1],
                            waits=W(t_nm, A_free[qb % 2] if kb4 == 0 else None))
                ps_bank[kb4] = t_exp
                exp_bk[qb][kb4] = t_exp
            exp_toks[qb] = t_exp
            t_as = vred(asum[:, :], asum4[:, 0:4], Alu.add, waits=[t_exp])
            vrecip(rsums[:, sb0 + qb:sb0 + qb + 1], asum[:, 0:1], waits=[t_as])
            if qb >= 3 and qb % 2 == 1:
                av_pair((qb - 3) // 2)
            if qb >= 1:
                a_transposes(qb - 1)
            if quant and qb in (2, 5, 8, 11, 14):
                quant_chunks([3 + (2, 5, 8, 11, 14).index(qb)])
                if qb == 14:
                    xrel_free[0] = kst_free[0]
                    xrel_free[1] = kst_free[1]
        a_transposes(15)
        av_pair(7)
        state["ps_free"] = ps_bank[3]
        state["A_free2"] = A_free
        state["AT_free"] = AT_free
        state["hT_ready"] = hT_last

        gparts = [[None] * 4 for _ in range(NQT)]
        gxa = [None] * NQT
        ps_half = [state["ps_free"], state["ps_free"]]

        def emit_gln_pre(qt, half):
            for tb4 in (2 * half, 2 * half + 1):
                tb = qt * 4 + tb4
                i = tb % 2
                r = row0 + tb * 128
                t_x = load(xrel[i], x_hbm[r:r + 128, :],
                           waits=W(xrel_free[i]))
                gparts[qt][tb4] = xn_pre(tb, xrel[i], t_x, "vector",
                                         stats_tok[qt], stb=sb0 + tb)
                xrel_free[i] = gparts[qt][tb4]

        def emit_gln_tps(qt, half):
            j = qt % 2
            for tb4 in (2 * half, 2 * half + 1):
                tb = qt * 4 + tb4
                gparts[qt][tb4] = xn_tps(tb, j, gparts[qt][tb4],
                                         PF[:, :],
                                         W(pf_free[0], pf_free[1]))
                pf_free[0] = gparts[qt][tb4][1]
                pf_free[1] = gparts[qt][tb4][1]

        def emit_xa(qt):
            j = qt % 2
            qcols = slice(qt * 512, (qt + 1) * 512)
            evs = [t for pair in gparts[qt] for t in pair]
            specs = []
            for c in range(8):
                specs.append((PA[0:16, 512:1024], a_h[:, c * 16:(c + 1) * 16],
                              xfh[j][:, c * 512:(c + 1) * 512], c == 0, False))
                specs.append((PA[0:16, 512:1024], a_h[:, c * 16:(c + 1) * 16],
                              xfl[j][:, c * 512:(c + 1) * 512], False, False))
            for c in range(8):
                specs.append((PA[0:16, 512:1024], a_l[:, c * 16:(c + 1) * 16],
                              xfh[j][:, c * 512:(c + 1) * 512], False, c == 7))
            t_xa = mm_group(specs, waits=W(*evs, pa_free[1], t_ah, t_alo))
            t_xah = scopy(xa_h[:, qcols], PA[0:16, 512:1024], waits=[t_xa])
            t_xal = tt("vector", xa_l[:, qcols], PA[0:16, 512:1024],
                       xa_h[:, qcols], Alu.subtract, waits=[t_xah])
            pa_free[1] = t_xal
            gxa[qt] = (t_xah, t_xal)

        def emit_ffn_chunks(qt, cs):
            j = qt % 2
            qcols = slice(qt * 512, (qt + 1) * 512)
            t_xah, t_xal = gxa[qt]
            for c in cs:
                half = c % 2
                csl = slice(c * 512, (c + 1) * 512)
                ps = PA[:, half * 512:(half + 1) * 512]
                specs = []
                for kc in range(8):
                    st = q_sb[:, kc * D + c * 128: kc * D + (c + 1) * 128]
                    specs.append((ps, st,
                                  xfh[j][:, kc * 512:(kc + 1) * 512],
                                  kc == 0, False))
                    specs.append((ps, st,
                                  xfl[j][:, kc * 512:(kc + 1) * 512],
                                  False, False))
                bsl = slice(c * 128, (c + 1) * 128)
                specs.append((ps, bs_h[:, bsl], xa_h[:, qcols], False, False))
                specs.append((ps, bs_h[:, bsl], xa_l[:, qcols], False, False))
                specs.append((ps, bs_l[:, bsl], xa_h[:, qcols], False, False))
                specs.append((ps, bias2[:, bsl], ones2[:, 0:512],
                              False, True))
                t_y = mm_group(specs, waits=W(pa_free[half], t_xah, t_xal,
                                              qtoks["t_qdone"],
                                              qtoks["t_bsh"], qtoks["t_bsl"],
                                              qtoks["t_bh"], qtoks["t_d2"],
                                              t_one2))
                t_g = act(g_sb[:, csl], ps, Act.Silu, scale=invsc[:, 1:2],
                          waits=W(t_y, state["g_free"]))
                pa_free[half] = t_g
                gparts[qt][0 if c < 4 else 1] = t_y
                gsilu[qt][c] = t_g

        def emit_wo_sg(qt):
            qcols = slice(qt * 512, (qt + 1) * 512)
            for c in range(8):
                half = c % 2
                csl = slice(c * 512, (c + 1) * 512)
                wsl = slice(c * 128, (c + 1) * 128)
                ps = PA[:, half * 512:(half + 1) * 512]
                specs = [
                    (ps, wo_h[:, wsl], hTh[:, qcols], True, False),
                    (ps, wo_h[:, wsl], hTl[:, qcols], False, False),
                    (ps, wo_l[:, wsl], hTh[:, qcols], False, True),
                ]
                t_s = mm_group(specs, waits=W(pa_free[half],
                                              state["hT_ready"],
                                              wsp[3] if c == 0 else None))
                t_sg = tt("vector", g_sb[:, csl], ps, g_sb[:, csl], Alu.mult,
                          waits=W(t_s, gsilu[qt][c]))
                pa_free[half] = t_sg
                gsilu[qt][c] = t_sg

        def emit_sgtp_stores(qt):
            j = qt % 2
            last_tp = None
            for tb4 in range(4):
                tb = qt * 4 + tb4
                i = tb % 2
                hp = tb4 % 2
                r = row0 + tb * 128
                t_xr = load(xrel[i], x_hbm[r:r + 128, :],
                            waits=W(xrel_free[i]))
                sp = [(PS[:, hp * 1024 + c * 128: hp * 1024 + (c + 1) * 128],
                       g_sb[:, c * 512 + tb4 * 128: c * 512 + tb4 * 128 + 128],
                       identf[:, :], c % 4 == 0, c % 4 == 3, True)
                      for c in range(8)]
                t_tp = mm_group(sp, waits=W(gsilu[qt][7], ps_half[hp]))
                last_tp = t_tp
                t_out = stt("vector", outb[i],
                            PS[:, hp * 1024:(hp + 1) * 1024],
                            rsums[:, sb0 + tb:sb0 + tb + 1], xrel[i],
                            Alu.mult, Alu.add,
                            waits=W(t_tp, t_xr, outb_free[i],
                                    xbuf_free[i]))
                ps_half[hp] = t_out
                xrel_free[i] = t_out
                t_st = store(y_hbm[r:r + 128, :], outb[i],
                             waits=[t_out])
                outb_free[i] = t_st
                xbuf_free[i] = t_st
                total_stores[0] += 1
            state["g_free"] = last_tp
            xf_free[j] = (gparts[qt][0], gparts[qt][1])

        def emit_prefetch_next():
            row0n = (s + 1) * S
            sb0n = (s + 1) * NTB
            srcs = [xbuf[2], xbuf[3], Vt[:, 0:1024], Vt[:, 1024:2048]]
            afree1 = state["A_free2"][1]
            lds = []
            for tb4 in range(4):
                r = row0n + tb4 * 128
                w = W(xbuf_free[SLOT[tb4]]) if tb4 < 2 else W(state["AT_free"])
                lds.append(load(srcs[tb4], x_hbm[r:r + 128, :], waits=w))
            sq_last = None
            sums = []
            for tb4 in range(4):
                col = slice(sb0n + tb4, sb0n + tb4 + 1)
                t_sum = vred(ssum[:, col], srcs[tb4], Alu.add,
                             waits=[lds[tb4]])
                t_sq = act(xn_tm[2 + tb4 % 2][:, :], srcs[tb4], Act.Square,
                           accum=ssq[:, col],
                           waits=W(lds[tb4], xntm_free[2 + tb4 % 2],
                                   afree1, sq_last))
                sq_last = t_sq
                sums.append((t_sum, t_sq))
            c0 = sb0n
            t1 = ts1("vector", smean[:, c0:c0 + 4], ssum[:, c0:c0 + 4],
                     1.0 / D, Alu.mult, waits=[sums[-1][0], sums[-1][1]])
            t2 = ts1("vector", svar[:, c0:c0 + 4], ssq[:, c0:c0 + 4],
                     1.0 / D, Alu.mult, waits=[t1])
            t3 = tt("vector", stmp[:, 0:4], smean[:, c0:c0 + 4],
                    smean[:, c0:c0 + 4], Alu.mult, waits=[t2])
            t4 = tt("vector", svar[:, c0:c0 + 4], svar[:, c0:c0 + 4],
                    stmp[:, 0:4], Alu.subtract, waits=[t3])
            t5 = act(svar[:, c0:c0 + 4], svar[:, c0:c0 + 4], Act.Ln,
                     bias=epsb[:, 0:1], waits=[t4, t_eps])
            t6 = act(srstd[:, c0:c0 + 4], svar[:, c0:c0 + 4], Act.Exp,
                     scale=-0.5, waits=[t5])
            t7 = stt("vector", snmr[:, c0:c0 + 4], smean[:, c0:c0 + 4], -1.0,
                     srstd[:, c0:c0 + 4], Alu.mult, Alu.mult, waits=[t6])
            ln01 = []
            for tb4 in range(2):
                t_xn = ts2("vector", xn_tm[2 + tb4][:, :], srcs[tb4],
                           smean[:, sb0n + tb4:sb0n + tb4 + 1],
                           srstd[:, sb0n + tb4:sb0n + tb4 + 1],
                           Alu.subtract, Alu.mult,
                           waits=W(t7, lds[tb4], sq_last, afree1))
                ln01.append(t_xn)
            state["pre1"] = {"stats": t7, "ln01": ln01,
                             "src23": srcs[2:], "ld23": lds[2:]}

        gsilu = [[None] * 8 for _ in range(NQT)]
        for it in range(NQT + 1):
            if it < NQT:
                emit_gln_pre(it, 0)
            if it > 0:
                emit_ffn_chunks(it - 1, [0, 1, 2, 3])
            if it < NQT:
                emit_gln_tps(it, 0)
                emit_gln_pre(it, 1)
            if it > 0:
                emit_ffn_chunks(it - 1, [4, 5, 6, 7])
            if it < NQT:
                emit_gln_tps(it, 1)
            if it > 0:
                emit_wo_sg(it - 1)
            if it < NQT:
                emit_xa(it)
            if it > 0:
                emit_sgtp_stores(it - 1)
            if it == 2 and s + 1 < SEQ_PER_CORE:
                emit_prefetch_next()
        state["ps_free"] = ps_half[1]
        state["ps_free_h"] = (ps_half[0], ps_half[1])

    for s in range(SEQ_PER_CORE):
        seq_program(s, quant=(s == 0))

    def final_wait(e):
        last = None
        for sem in s_st:
            if sem.n > 0:
                last = e.wait_ge(sem.h, sem.n)
        return last
    plan.emit("scalar", final_wait, (), None)

    with ctx:
        for sem in sems:
            sem.h = ctx.enter_context(nc.semaphore(sem.name))

        def runner(stream):
            def run(e):
                last_wait = {}
                for waits, build, inc, amt in stream:
                    for sem, v in waits:
                        if v is None or v <= 0:
                            continue
                        if last_wait.get(sem.name, -1) >= v:
                            continue
                        e.wait_ge(sem.h, v)
                        last_wait[sem.name] = v
                    inst = build(e)
                    if inc is not None:
                        inst.then_inc(inc.h, amt)
            return run

        with nc.Block() as block:
            block.sync(runner(plan.streams["sync"]))
            block.tensor(runner(plan.streams["tensor"]))
            block.vector(runner(plan.streams["vector"]))
            block.scalar(runner(plan.streams["scalar"]))
            block.gpsimd(runner(plan.streams["gpsimd"]))
    return nc


_NC_CACHE = None


def kernel(L0_attn_x, L0_attn_head_dim_mask, L0_attn_w_q, L0_attn_w_k,
           L0_attn_w_v, L0_attn_w_o, L0_ffn_kernel, L0_ffn_bias, L0_ffn_a,
           L0_ffn_b):
    global _NC_CACHE
    x = np.ascontiguousarray(np.asarray(L0_attn_x, dtype=np.float32))
    mask = np.ascontiguousarray(
        np.asarray(L0_attn_head_dim_mask, dtype=np.int32).reshape(128, 1))
    wq = np.ascontiguousarray(np.asarray(L0_attn_w_q, dtype=np.float32))
    wk = np.ascontiguousarray(np.asarray(L0_attn_w_k, dtype=np.float32))
    wv = np.ascontiguousarray(np.asarray(L0_attn_w_v, dtype=np.float32))
    wo = np.ascontiguousarray(np.asarray(L0_attn_w_o, dtype=np.float32))
    kern = np.ascontiguousarray(np.asarray(L0_ffn_kernel, dtype=np.float32))
    fbias = np.ascontiguousarray(
        np.asarray(L0_ffn_bias, dtype=np.float32).reshape(1, D))
    la = np.ascontiguousarray(np.asarray(L0_ffn_a, dtype=np.float32))
    lb = np.ascontiguousarray(np.asarray(L0_ffn_b, dtype=np.float32))

    if _NC_CACHE is None:
        _NC_CACHE = build_nc()
    nc = _NC_CACHE

    in_maps = []
    for c in range(NCORES):
        xs = np.ascontiguousarray(
            x[c * SEQ_PER_CORE:(c + 1) * SEQ_PER_CORE].reshape(
                SEQ_PER_CORE * S, D))
        in_maps.append({
            "x": xs, "mask": mask, "wq": wq, "wk": wk, "wv": wv, "wo": wo,
            "kern": kern, "fbias": fbias, "la": la, "lb": lb,
        })
    res = run_bass_kernel_spmd(nc, in_maps, core_ids=list(range(NCORES)))
    out = np.concatenate([np.asarray(r["y"]).reshape(SEQ_PER_CORE, S, D)
                          for r in res.results], axis=0)
    return out.astype(np.float32)


# version 24
# speedup vs baseline: 1.0106x; 1.0011x over previous
import contextlib

import numpy as np

import concourse.bass as bass
import concourse.mybir as mybir
from concourse.bass_utils import run_bass_kernel_spmd

F32 = mybir.dt.float32
BF16 = mybir.dt.bfloat16
I32 = mybir.dt.int32
Alu = mybir.AluOpType
Act = mybir.ActivationFunctionType
AX = mybir.AxisListType

NCORES = 8
SEQ_PER_CORE = 2
S = 2048
D = 1024
NTB = S // 128
NQT = S // 512
NKC = D // 128
C_ATT = float(np.float32(1.0) / np.sqrt(np.float32(128.0)))
LN_EPS = 1e-5

ENGS = ("sync", "tensor", "vector", "scalar", "gpsimd")


class Sem:
    __slots__ = ("name", "n", "h")

    def __init__(self, name):
        self.name = name
        self.n = 0
        self.h = None

    def tok(self):
        return (self, self.n)


class Plan:
    def __init__(self):
        self.streams = {e: [] for e in ENGS}

    def emit(self, eng, build, waits=(), inc=None, amt=1):
        self.streams[eng].append((tuple(waits), build, inc, amt))
        if inc is not None:
            inc.n += amt
        return inc.tok() if inc is not None else None


def build_nc():
    nc = bass.Bass("TRN2", target_bir_lowering=False)

    x_hbm = nc.dram_tensor("x", [SEQ_PER_CORE * S, D], F32, kind="ExternalInput")
    mask_hbm = nc.dram_tensor("mask", [128, 1], I32, kind="ExternalInput")
    wq_hbm = nc.dram_tensor("wq", [D, 128], F32, kind="ExternalInput")
    wk_hbm = nc.dram_tensor("wk", [D, 128], F32, kind="ExternalInput")
    wv_hbm = nc.dram_tensor("wv", [D, 128], F32, kind="ExternalInput")
    wo_hbm = nc.dram_tensor("wo", [128, D], F32, kind="ExternalInput")
    kern_hbm = nc.dram_tensor("kern", [D, D], F32, kind="ExternalInput")
    bias_hbm = nc.dram_tensor("fbias", [1, D], F32, kind="ExternalInput")
    a_hbm = nc.dram_tensor("la", [D, 16], F32, kind="ExternalInput")
    b_hbm = nc.dram_tensor("lb", [16, D], F32, kind="ExternalInput")
    y_hbm = nc.dram_tensor("y", [SEQ_PER_CORE * S, D], F32, kind="ExternalOutput")

    plan = Plan()
    s_pe = Sem("s_pe")
    s_dve = Sem("s_dve")
    s_act = Sem("s_act")
    s_gp = Sem("s_gp")
    s_load = [Sem(f"s_load{i}") for i in range(6)]
    s_kld = [Sem(f"s_kld{i}") for i in range(2)]
    s_st = [Sem(f"s_st{i}") for i in range(3)]
    sems = [s_pe, s_dve, s_act, s_gp] + s_load + s_st + s_kld
    dma_idx = {"load": 0, "store": 0, "kload": 0}

    ctx = contextlib.ExitStack()

    def sb(name, shape, dtype):
        return ctx.enter_context(nc.sbuf_tensor(name, shape, dtype))

    iof = sb("iof", [128, 128], I32)
    iop = sb("iop", [128, 128], I32)
    identf = sb("identf", [128, 128], F32)
    onesr = sb("onesr", [1, 512], F32)
    maskr = sb("maskr", [128, 1], I32)
    maskf = sb("maskf", [128, 1], F32)
    biasr = sb("biasr", [1, D], F32)
    wstpair = sb("wstpair", [128, 2 * D], F32)
    wst = [wstpair[:, 0:D], wstpair[:, D:2 * D]]
    wq_h = sb("wq_h", [128, D], BF16); wq_l = sb("wq_l", [128, D], BF16)
    wk_h = sb("wk_h", [128, D], BF16); wk_l = sb("wk_l", [128, D], BF16)
    wv_h = sb("wv_h", [128, D], BF16); wv_l = sb("wv_l", [128, D], BF16)
    wo_h = sb("wo_h", [128, D], BF16); wo_l = sb("wo_l", [128, D], BF16)
    q_sb = sb("q_sb", [128, NKC * D], BF16)
    a_st = sb("a_st", [128, 128], F32)
    a_h = sb("a_h", [128, 128], BF16); a_l = sb("a_l", [128, 128], BF16)
    bs_h = sb("bs_h", [16, D], BF16); bs_l = sb("bs_l", [16, D], BF16)
    ones2 = sb("ones2", [2, 512], BF16)
    bias2 = sb("bias2", [2, D], BF16)
    km16 = sb("km16", [128, 16], F32)
    kmaxc = sb("kmaxc", [128, 1], F32)
    krow = sb("krow", [1, 128], F32)
    ksc = sb("ksc", [1, 2], F32)
    invsc = sb("invsc", [128, 2], F32)
    kti = sb("kti", [128, D], I32)
    m1b = sb("m1b", [128, 1], F32)
    ssum = sb("ssum", [128, 2 * NTB], F32)
    ssq = sb("ssq", [128, 2 * NTB], F32)
    smean = sb("smean", [128, 2 * NTB], F32)
    svar = sb("svar", [128, 2 * NTB], F32)
    srstd = sb("srstd", [128, 2 * NTB], F32)
    stmp = sb("stmp", [128, 4], F32)
    snmr = sb("snmr", [128, 2 * NTB], F32)
    rsums = sb("rsums", [128, 2 * NTB], F32)
    nmax = sb("nmax", [128, 1], F32)
    nm4 = sb("nm4", [128, 4], F32)
    asum = sb("asum", [128, 1], F32)
    asum4 = sb("asum4", [128, 4], F32)
    epsb = sb("epsb", [128, 1], F32)
    catt = sb("catt", [1, 2], F32)
    cattb = sb("cattb", [128, 1], F32)
    xq = sb("xq", [128, 4 * D], F32)
    xbuf = [xq[:, i * D:(i + 1) * D] for i in range(4)]
    xn_tm = [sb(f"xn_tm{i}", [128, D], F32) for i in range(2)]
    xfh = [sb(f"xfh{i}", [128, NKC * 512], BF16) for i in range(2)]
    xfl = [sb(f"xfl{i}", [128, NKC * 512], BF16) for i in range(2)]
    Qh = sb("Qh", [128, S], BF16); Ql = sb("Ql", [128, S], BF16)
    Kh = sb("Kh", [128, S], BF16); Kl = sb("Kl", [128, S], BF16)
    Vf = sb("Vf", [128, 512], F32)
    Vt = sb("Vt", [128, S], F32)
    A_sb = [sb(f"A_sb{i}", [128, S], F32) for i in range(2)]
    ATf = sb("ATf", [128, 16 * 256], F32)
    hTh = sb("hTh", [128, S], BF16); hTl = sb("hTl", [128, S], BF16)
    xa_h = sb("xa_h", [16, S], BF16); xa_l = sb("xa_l", [16, S], BF16)
    g_sb = sb("g_sb", [128, NKC * 512], F32)
    xrel = wst
    outb = xbuf
    A1_addr = nc.lookup_mloc(A_sb[1]).addr
    xn_tm.append(nc.alloc_sbuf_tensor_at("xn_tm2", [128, D], F32,
                                         offset=A1_addr))
    xn_tm.append(nc.alloc_sbuf_tensor_at("xn_tm3", [128, D], F32,
                                         offset=A1_addr + 4096))

    g_addr = nc.lookup_mloc(g_sb).addr
    kq_y = nc.alloc_sbuf_tensor_at("kq_y", [128, D], F32, offset=g_addr)
    ktf = nc.alloc_sbuf_tensor_at("ktf", [128, D], F32, offset=g_addr + 4096)
    kdd = nc.alloc_sbuf_tensor_at("kdd", [128, D], F32, offset=g_addr + 8192)
    b_st = nc.alloc_sbuf_tensor_at("b_st", [16, D], F32, offset=g_addr + 12288)
    blrow = nc.alloc_sbuf_tensor_at("blrow", [1, D], BF16,
                                    offset=g_addr + 14336)

    PF = ctx.enter_context(nc.psum_tensor("PF", [128, 1024], F32))
    PA = ctx.enter_context(nc.psum_tensor("PA", [128, 1024], F32))
    PS = ctx.enter_context(nc.psum_tensor("PS", [128, 2048], F32))

    ESEM = {"vector": s_dve, "gpsimd": s_gp, "scalar": s_act, "tensor": s_pe}

    def mk_mm(out, lhsT, rhs, start, stop, is_t=False):
        return (lambda e, o=out, l=lhsT, r=rhs, st=start, sp=stop, t=is_t:
                e.matmul(o, l, r, start=st, stop=sp, is_transpose=t))

    def mm_group(specs, waits, ):
        tok = None
        n = len(specs)
        for i, sp in enumerate(specs):
            is_t = sp[5] if len(sp) > 5 else False
            tok = plan.emit(
                "tensor", mk_mm(sp[0], sp[1], sp[2], sp[3], sp[4], is_t),
                waits=waits if i == 0 else (),
                inc=s_pe if i == n - 1 else None)
        return tok

    def ts1(eng, out, in0, sc, op, waits=()):
        return plan.emit(
            eng, lambda e, o=out, i=in0, s=sc, p=op: e.tensor_scalar(o, i, s, None, p),
            waits, ESEM[eng])

    def ts2(eng, out, in0, s1, s2, op1, op2, waits=()):
        return plan.emit(
            eng,
            lambda e, o=out, i=in0, a=s1, b=s2, p=op1, q=op2: e.tensor_scalar(o, i, a, b, p, q),
            waits, ESEM[eng])

    def stt(eng, out, in0, sc, in1, op0, op1, waits=()):
        return plan.emit(
            eng,
            lambda e, o=out, i=in0, s=sc, j=in1, p=op0, q=op1: e.scalar_tensor_tensor(o, i, s, j, p, q),
            waits, ESEM[eng])

    def tt(eng, out, in0, in1, op, waits=()):
        return plan.emit(
            eng, lambda e, o=out, i=in0, j=in1, p=op: e.tensor_tensor(o, i, j, p),
            waits, ESEM[eng])

    def cp(eng, out, in0, waits=()):
        if eng == "scalar":
            return plan.emit(eng, lambda e, o=out, i=in0: e.copy(o, i),
                             waits, s_act)
        return plan.emit(eng, lambda e, o=out, i=in0: e.tensor_copy(o, i),
                         waits, ESEM[eng])

    def act(out, in0, func, bias=0.0, scale=1.0, accum=None, waits=()):
        return plan.emit(
            "scalar",
            (lambda e, o=out, i=in0, f=func, b=bias, s=scale, a=accum:
             e.activation(o, i, f, bias=b, scale=s, accum_out=a)),
            waits, s_act)

    def scopy(out, in0, scale=1.0, waits=()):
        return act(out, in0, Act.Copy, bias=0.0, scale=scale, waits=waits)

    def _dma(kind, eng, pool, dst, src, waits):
        sem = pool[dma_idx[kind] % len(pool)]
        dma_idx[kind] += 1
        w = list(waits)
        if sem.n > 0:
            w.append((sem, sem.n))
        return plan.emit(eng, lambda e, d=dst, s=src: e.dma_start(d, s),
                         w, sem, 16)

    def load(dst, src, waits=()):
        return _dma("load", "sync", s_load, dst, src, waits)

    def kload(dst, src, waits=()):
        return _dma("kload", "gpsimd", s_kld, dst, src, waits)

    def store(dst, src, waits=()):
        return _dma("store", "scalar", s_st, dst, src, waits)

    def vred(out, in0, op, negate=False, absval=False, waits=()):
        def b(e, o=out, i=in0, p=op, n=negate, a=absval):
            return e.tensor_reduce(o, i, AX.X, p, apply_absolute_value=a,
                                   negate=n)
        return plan.emit("vector", b, waits, s_dve)

    def vrecip(out, in0, waits=()):
        return plan.emit("vector", lambda e, o=out, i=in0: e.reciprocal(o, i),
                         waits, s_dve)

    def memset(eng, ap, v, waits=()):
        return plan.emit(eng, lambda e, a=ap, x=v: e.memset(a, x),
                         waits, ESEM[eng])

    def W(*toks):
        return [t for t in toks if t is not None]

    t_iota = plan.emit(
        "gpsimd",
        lambda e: (e.iota(iof[:, :], [[1, 128]], channel_multiplier=0),
                   e.iota(iop[:, :], [[0, 128]], channel_multiplier=1))[-1],
        (), s_gp)
    t_identf = tt("vector", identf[:, :], iof[:, :], iop[:, :], Alu.is_equal,
                  waits=[t_iota])
    t_ones = memset("vector", onesr[:, :], 1.0)
    t_eps = memset("vector", epsb[:, :], LN_EPS)
    t_one2 = memset("vector", ones2[:, :], 1.0)
    t_c0 = memset("vector", catt[0:1, 0:1], 128.0)
    t_c1 = act(catt[0:1, 1:2], catt[0:1, 0:1], Act.Sqrt, waits=[t_c0])
    t_c2 = vrecip(catt[0:1, 0:1], catt[0:1, 1:2], waits=[t_c1])

    SLOT = (2, 3, 0, 1)
    pre_x = [load(xbuf[SLOT[tb4]], x_hbm[tb4 * 128:(tb4 + 1) * 128, :])
             for tb4 in range(4)]

    t_ml = load(maskr[:, :], mask_hbm[:, :])
    t_maskf = cp("vector", maskf[:, :], maskr[:, :], waits=[t_ml])

    t_bl = load(biasr[:, :], bias_hbm[:, :])
    t_al = load(a_st.rearrange("p (c r) -> p c r", c=NKC),
                a_hbm.rearrange("(c p) r -> p c r", p=128))
    t_bload = load(b_st[:, :], b_hbm[:, :])
    t_ah = cp("vector", a_h[:, :], a_st[:, :], waits=[t_al])
    t_alo = tt("vector", a_l[:, :], a_st[:, :], a_h[:, :], Alu.subtract,
               waits=[t_ah])

    wsp = []
    w_free = [None, None]
    for i, (hbm, hi, lo, is_wo) in enumerate([
            (wq_hbm, wq_h, wq_l, False), (wk_hbm, wk_h, wk_l, False),
            (wv_hbm, wv_h, wv_l, False), (wo_hbm, wo_h, wo_l, True)]):
        stg = wst[i % 2]
        w = W(w_free[i % 2])
        if is_wo:
            t_ld = load(stg, hbm[:, :], waits=w)
            t_m = ts1("vector", stg, stg, maskf[:, 0:1], Alu.mult,
                      waits=[t_ld, t_maskf])
            t_hi = cp("gpsimd", hi[:, :], stg, waits=[t_m])
        else:
            t_ld = load(stg.rearrange("p (c n) -> p c n", c=NKC),
                        hbm.rearrange("(c p) n -> p c n", p=128), waits=w)
            t_hi = cp("gpsimd", hi[:, :], stg, waits=[t_ld])
        t_lo = tt("gpsimd", lo[:, :], stg, hi[:, :], Alu.subtract,
                  waits=[t_hi])
        w_free[i % 2] = t_lo
        wsp.append(t_lo)

    t_cb = mm_group([(PA[:, 0:1], onesr[0:1, 0:128], catt[0:1, 0:1], True,
                      True)], waits=[t_c2, t_ones, t_identf])
    t_cbe = cp("vector", cattb[:, :], PA[:, 0:1], waits=[t_cb])

    t_m1b = memset("vector", m1b[:, :], -1.0)

    kst_free = [w_free[0], w_free[1]]
    qtoks = {}

    def quant_stage0():
        last_r = None
        for c in range(NKC):
            stg = wst[c % 2]
            t_ld = kload(stg, kern_hbm[c * 128:(c + 1) * 128, :],
                         waits=W(kst_free[c % 2]))
            t_r = vred(km16[:, c:c + 1], stg, Alu.max, absval=True,
                       waits=[t_ld])
            kst_free[c % 2] = t_r
            last_r = t_r
        qtoks["absmax"] = last_r

    def quant_stage1(pa0_guard):
        t_km = vred(kmaxc[:, :], km16[:, 0:NKC], Alu.max,
                    waits=[qtoks["absmax"]])
        t_tp0 = mm_group([(PA[0:1, 0:128], kmaxc[:, :], identf[:, :], True,
                           True, True)], waits=W(t_km, pa0_guard))
        t_krow = cp("vector", krow[:, :], PA[0:1, 0:128], waits=[t_tp0])
        t_kM = vred(ksc[0:1, 0:1], krow[:, :], Alu.max, waits=[t_krow])
        t_i = vrecip(stmp[0:1, 0:1], ksc[0:1, 0:1], waits=[t_kM])
        t_inv = ts1("vector", ksc[0:1, 0:1], stmp[0:1, 0:1], 7.0, Alu.mult,
                    waits=[t_i])
        t_sc = vrecip(ksc[0:1, 1:2], ksc[0:1, 0:1], waits=[t_inv])
        t_bc = mm_group([(PA[:, 0:2], onesr[0:1, 0:128], ksc[0:1, 0:2], True,
                          True)], waits=[t_sc, t_krow])
        t_is = cp("vector", invsc[:, :], PA[:, 0:2], waits=[t_bc])
        t_be = ts1("vector", biasr[:, :], biasr[:, :], invsc[0:1, 0:1],
                   Alu.mult, waits=[t_bl, t_is])
        t_bsh = ts1("vector", bs_h[:, :], b_st[:, :], invsc[0:16, 0:1],
                    Alu.mult, waits=[t_bload, t_is])
        t_bsl = stt("vector", bs_l[:, :], b_st[:, :], invsc[0:16, 0:1],
                    bs_h[:, :], Alu.mult, Alu.subtract, waits=[t_bsh])
        t_bh = cp("vector", bias2[0:1, :], biasr[:, :], waits=[t_be, t_bsl])
        t_blo = tt("vector", blrow[:, :], biasr[:, :], bias2[0:1, :],
                   Alu.subtract, waits=[t_bh])
        t_d2 = load(bias2[1:2, :], blrow[:, :], waits=[t_blo])
        qtoks.update(t_bsh=t_bsh, t_bsl=t_bsl, t_be=t_be, t_is=t_is,
                     t_bh=t_bh, t_d2=t_d2)
        return t_is

    def quant_chunks(cs):
        t_is = qtoks["t_is"]
        for c in cs:
            stg = wst[c % 2]
            t_ld = kload(stg, kern_hbm[c * 128:(c + 1) * 128, :],
                         waits=W(kst_free[c % 2]))
            qdst = q_sb[:, c * D:(c + 1) * D]
            u1 = ts1("vector", kq_y[:, :], stg, invsc[:, 0:1], Alu.mult,
                     waits=W(t_ld, t_is, qtoks.get("qlast")))
            u2 = cp("gpsimd", kti[:, :], kq_y[:, :], waits=[u1])
            u3 = ts1("vector", ktf[:, :], kti[:, :], 1.0, Alu.subtract,
                     waits=[u2])
            u4 = stt("vector", kdd[:, :], ktf[:, :], 1.0, kq_y[:, :],
                     Alu.add, Alu.subtract, waits=[u3])
            u5 = stt("vector", kq_y[:, :], kdd[:, :], -0.5, ktf[:, :],
                     Alu.is_le, Alu.add, waits=[u4])
            u6 = stt("vector", qdst, kdd[:, :], 0.5, kq_y[:, :],
                     Alu.is_le, Alu.add, waits=[u5])
            kst_free[c % 2] = u6
            qtoks["qlast"] = u6
            qtoks["t_qdone"] = u6

    xbuf_free = [None, None, None, None]
    xntm_free = [None, None, None, None]
    xf_free = [(None, None), (None, None)]
    pf_free = [t_identf, t_identf]
    ab_ps = [t_identf, t_identf]
    pa_free = [t_cbe, None]
    state = {
        "ps_free": None, "A_free": None, "AT_free": None,
        "Vt_ready": None, "QK_ready": None, "hT_ready": None,
        "g_free": None,
    }
    xrel_free = [None, None]
    outb_free = [None, None]
    total_stores = [0]

    def xn_pre(tb, src, src_tok, eng, stats_wait, i=None, stb=None):
        if i is None:
            i = tb % 2
        if stb is None:
            stb = tb
        t_xn = ts2("vector", xn_tm[i][:, :], src,
                   smean[:, stb:stb + 1], srstd[:, stb:stb + 1],
                   Alu.subtract, Alu.mult,
                   waits=W(src_tok, stats_wait, xntm_free[i]))
        return t_xn

    def xn_tps(tb, jbuf, t_xn, psd, ps_guard, i=None):
        if not isinstance(ps_guard, list):
            ps_guard = [ps_guard]
        """psd: [128, 1024] fp32 psum view (2 banks). Returns
        (t_eh, t_el)."""
        if i is None:
            i = tb % 2
        tb4 = tb % 4
        sp = [(psd[:, c * 128:(c + 1) * 128],
               xn_tm[i][:, c * 128:(c + 1) * 128], identf[:, :],
               c % 4 == 0, c % 4 == 3, True) for c in range(8)]
        t_tp = mm_group(sp, waits=W(t_xn, *ps_guard))
        xntm_free[i] = t_tp
        dsth = xfh[jbuf].rearrange("p (c t) -> p c t", t=512)[:, :, tb4 * 128:(tb4 + 1) * 128]
        dstl = xfl[jbuf].rearrange("p (c t) -> p c t", t=512)[:, :, tb4 * 128:(tb4 + 1) * 128]
        srcv = psd.rearrange("p (c t) -> p c t", t=128)
        t_eh = cp("scalar", dsth, srcv, waits=W(t_tp, xf_free[jbuf][0]))
        t_el = plan.emit(
            "vector",
            (lambda e, o=dstl, ii=srcv, hh=dsth:
             e.tensor_tensor(o, ii, hh, Alu.subtract)),
            W(t_eh, xf_free[jbuf][1]), s_dve)
        return t_eh, t_el

    def seq_program(s, quant=False):
        row0 = s * S
        sb0 = s * NTB

        stats_tok = [None] * NQT
        ln_toks = [[None] * 4 for _ in range(NQT)]
        pre = state.pop("pre1", None)
        if pre is not None:
            ab_ps[0] = state["ps_free_h"][0]
            ab_ps[1] = state["ps_free_h"][1]

        def emit_ln_loads_stats(qt):
            if qt == 0 and pre is not None:
                stats_tok[0] = pre["stats"]
                return
            if s == 0 and qt == 0:
                sq_toks = list(pre_x)
            else:
                sq_toks = []
                for tb4 in range(4):
                    r = row0 + (qt * 4 + tb4) * 128
                    sq_toks.append(load(xbuf[SLOT[tb4]], x_hbm[r:r + 128, :],
                                        waits=W(xbuf_free[SLOT[tb4]])))
            sums = []
            for tb4 in range(4):
                tb = qt * 4 + tb4
                sl = SLOT[tb4]
                t_sum = vred(ssum[:, sb0 + tb:sb0 + tb + 1], xbuf[sl], Alu.add,
                             waits=[sq_toks[tb4]])
                t_sq = act(xn_tm[sl][:, :], xbuf[sl], Act.Square,
                           accum=ssq[:, sb0 + tb:sb0 + tb + 1],
                           waits=W(sq_toks[tb4], xntm_free[sl]))
                xntm_free[sl] = t_sq
                sums.append((t_sum, t_sq))
            c0 = qt * 4
            t1 = ts1("vector", smean[:, sb0 + c0:sb0 + c0 + 4], ssum[:, sb0 + c0:sb0 + c0 + 4],
                     1.0 / D, Alu.mult, waits=[sums[-1][0], sums[-1][1]])
            t2 = ts1("vector", svar[:, sb0 + c0:sb0 + c0 + 4], ssq[:, sb0 + c0:sb0 + c0 + 4],
                     1.0 / D, Alu.mult, waits=[t1])
            t3 = tt("vector", stmp[:, 0:4], smean[:, sb0 + c0:sb0 + c0 + 4],
                    smean[:, sb0 + c0:sb0 + c0 + 4], Alu.mult, waits=[t2])
            t4 = tt("vector", svar[:, sb0 + c0:sb0 + c0 + 4], svar[:, sb0 + c0:sb0 + c0 + 4],
                    stmp[:, 0:4], Alu.subtract, waits=[t3])
            t5 = act(svar[:, sb0 + c0:sb0 + c0 + 4], svar[:, sb0 + c0:sb0 + c0 + 4], Act.Ln,
                     bias=epsb[:, 0:1], waits=[t4, t_eps])
            t6 = act(srstd[:, sb0 + c0:sb0 + c0 + 4], svar[:, sb0 + c0:sb0 + c0 + 4], Act.Exp,
                     scale=-0.5, waits=[t5])
            t7 = stt("vector", snmr[:, sb0 + c0:sb0 + c0 + 4], smean[:, sb0 + c0:sb0 + c0 + 4], -1.0,
                     srstd[:, sb0 + c0:sb0 + c0 + 4], Alu.mult, Alu.mult, waits=[t6])
            stats_tok[qt] = t7

        def emit_ln_xn(qt, half):
            if qt == 0 and pre is not None:
                if half == 0:
                    ln_toks[0][0] = pre["ln01"][0]
                    ln_toks[0][1] = pre["ln01"][1]
                    xbuf_free[2] = pre["ln01"][0]
                    xbuf_free[3] = pre["ln01"][1]
                else:
                    for tb4 in (2, 3):
                        sl = SLOT[tb4]
                        ln_toks[0][tb4] = xn_pre(
                            tb4, pre["src23"][tb4 - 2],
                            pre["ld23"][tb4 - 2], "vector",
                            stats_tok[0], i=sl, stb=sb0 + tb4)
                return
            for tb4 in (2 * half, 2 * half + 1):
                tb = qt * 4 + tb4
                sl = SLOT[tb4]
                ln_toks[qt][tb4] = xn_pre(tb, xbuf[sl], None, "vector",
                                          stats_tok[qt], i=sl,
                                          stb=sb0 + tb)
                xbuf_free[sl] = ln_toks[qt][tb4]

        def emit_ln_tps(qt, half):
            j = qt % 2
            for tb4 in (2 * half, 2 * half + 1):
                tb = qt * 4 + tb4
                hp = tb4 % 2
                ln_toks[qt][tb4] = xn_tps(
                    tb, j, ln_toks[qt][tb4],
                    PS[:, hp * 1024:(hp + 1) * 1024], ab_ps[hp],
                    i=SLOT[tb4])
                ab_ps[hp] = ln_toks[qt][tb4][1]

        def proj(qt, wh, wl, half, extra, wtok):
            j = qt % 2
            ehs = [p[0] for p in ln_toks[qt]]
            els = [p[1] for p in ln_toks[qt]]
            ps = PA[:, half * 512:(half + 1) * 512]
            for c in range(8):
                plan.emit("tensor",
                          mk_mm(ps, wh[:, c * 128:(c + 1) * 128],
                                xfh[j][:, c * 512:(c + 1) * 512],
                                c == 0, False),
                          waits=W(*ehs, *extra, wtok) if c == 0 else ())
            for c in range(8):
                plan.emit("tensor",
                          mk_mm(ps, wl[:, c * 128:(c + 1) * 128],
                                xfh[j][:, c * 512:(c + 1) * 512],
                                False, False))
            tok = None
            for c in range(8):
                tok = plan.emit("tensor",
                                mk_mm(ps, wh[:, c * 128:(c + 1) * 128],
                                      xfl[j][:, c * 512:(c + 1) * 512],
                                      False, c == 7),
                                waits=W(*els) if c == 0 else (),
                                inc=s_pe if c == 7 else None)
            return tok

        def emit_q(qt):
            qcols = slice(qt * 512, (qt + 1) * 512)
            t_q = proj(qt, wq_h, wq_l, 0, W(pa_free[0]), wsp[0])
            t_qh = scopy(Qh[:, qcols], PA[:, 0:512], scale=cattb[:, 0:1],
                         waits=[t_q, t_cbe])
            t_ql = stt("vector", Ql[:, qcols], PA[:, 0:512], cattb[:, 0:1],
                       Qh[:, qcols], Alu.mult, Alu.subtract,
                       waits=[t_qh, t_cbe])
            pa_free[0] = t_ql
            if qt == NQT - 1:
                state["Q_ready"] = t_ql

        def emit_k(qt):
            qcols = slice(qt * 512, (qt + 1) * 512)
            t_k = proj(qt, wk_h, wk_l, 1, W(pa_free[1]), wsp[1])
            t_kh = scopy(Kh[:, qcols], PA[:, 512:1024], waits=[t_k])
            t_kl = tt("vector", Kl[:, qcols], PA[:, 512:1024], Kh[:, qcols],
                      Alu.subtract, waits=[t_kh])
            pa_free[1] = t_kl
            if qt == NQT - 1:
                state["QK_ready"] = (state["Q_ready"], t_kl)

        vtoks = {}

        def emit_v(qt):
            j = qt % 2
            t_v = proj(qt, wv_h, wv_l, 0, W(pa_free[0]), wsp[2])
            t_vf = cp("vector", Vf[:, :], PA[:, 0:512], waits=[t_v])
            pa_free[0] = t_vf
            xf_free[j] = (t_v, t_v)
            vtoks[qt] = t_vf

        def emit_vtp(qt):
            qcols = slice(qt * 512, (qt + 1) * 512)
            t_vf = vtoks[qt]
            sp = [(PF[:, tb4 * 128:(tb4 + 1) * 128],
                   Vf[:, tb4 * 128:(tb4 + 1) * 128], identf[:, :],
                   tb4 % 4 == 0, tb4 % 4 == 3, True) for tb4 in range(4)]
            t_tv = mm_group(sp, waits=W(t_vf, pf_free[0]))
            t_ev = scopy(Vt[:, qcols], PF[:, 0:512], waits=[t_tv])
            pf_free[0] = t_ev
            if qt == NQT - 1:
                state["Vt_ready"] = (t_ev, t_ev)

        for it in range(NQT + 1):
            if it < NQT:
                emit_ln_loads_stats(it)
            if it > 0:
                emit_q(it - 1)
            if it < NQT:
                emit_ln_xn(it, 0)
            if it > 0:
                emit_k(it - 1)
            if it < NQT:
                emit_ln_xn(it, 1)
            if it > 0:
                emit_v(it - 1)
            if it < NQT:
                emit_ln_tps(it, 0)
                emit_ln_tps(it, 1)
            if it > 0:
                emit_vtp(it - 1)
            if quant:
                if it == 1:
                    quant_stage0()
                elif it == 2:
                    pa_free[0] = quant_stage1(pa_free[0])
                    quant_chunks([0])
                elif it == 3:
                    quant_chunks([1])
                elif it == 4:
                    quant_chunks([2])

        A_free = state.get("A_free2", [None, None])
        AT_free = state["AT_free"]
        hT_last = None
        exp_toks = [None] * 16
        exp_bk = [[None] * 4 for _ in range(16)]

        def a_transposes(qb):
            nonlocal AT_free
            qh2 = qb % 2
            ab = A_sb[qb % 2]
            last_l = None
            tgts = [(PA, 0, pa_free), (PA, 1, pa_free),
                    (PF, 0, pf_free), (PF, 1, pf_free)]
            for rr in range(4):
                pt, half, freelist = tgts[rr]
                sp = [(pt[:, half * 512 + kk * 128: half * 512 + (kk + 1) * 128],
                       ab[:, (4 * rr + kk) * 128:(4 * rr + kk + 1) * 128],
                       identf[:, :], kk == 0, kk == 3, True)
                      for kk in range(4)]
                t_tp = mm_group(sp, waits=W(exp_bk[qb][rr], freelist[half],
                                            AT_free))
                dsth = ATf.rearrange("p (kb q) -> p kb q", q=256)[:, 4 * rr:4 * rr + 4, qh2 * 128:(qh2 + 1) * 128]
                srcv = pt.rearrange("p (h kb q) -> p h kb q", h=2, q=128)[:, half, :, :]
                t_h = cp("scalar", dsth, srcv, waits=[t_tp])
                freelist[half] = t_h
                last_l = t_h
            A_free[qb % 2] = last_l

        def av_pair(pair):
            nonlocal AT_free, hT_last
            quarter = pair % 2
            po = quarter * 512
            specs = []
            for kb in range(16):
                vsl = slice(kb * 128, (kb + 1) * 128)
                asl = slice(kb * 256, (kb + 1) * 256)
                specs.append((PA[:, po:po + 256], Vt[:, vsl],
                              ATf[:, asl], kb == 0, kb == 15))
            t_av = mm_group(specs,
                            waits=W(A_free[1], state["Vt_ready"][0],
                                    state["Vt_ready"][1], pa_free[quarter]))
            AT_free = t_av
            hsl = slice(pair * 256, (pair + 1) * 256)
            t_hh = scopy(hTh[:, hsl], PA[:, po:po + 256], waits=[t_av])
            t_hl = tt("vector", hTl[:, hsl], PA[:, po:po + 256],
                      hTh[:, hsl], Alu.subtract, waits=[t_hh])
            pa_free[quarter] = t_hl
            hT_last = t_hl

        ps_bank = [ab_ps[0], ab_ps[0], ab_ps[1], ab_ps[1]]
        for qb in range(16):
            qsl = slice(qb * 128, (qb + 1) * 128)
            pm_last = None
            for kb4 in range(4):
                ks = slice(kb4 * 512, (kb4 + 1) * 512)
                plan.emit("tensor", mk_mm(PS[:, ks], Qh[:, qsl], Kh[:, ks],
                                          True, False),
                          waits=W(state["QK_ready"][0], state["QK_ready"][1],
                                  ps_bank[kb4]))
                plan.emit("tensor", mk_mm(PS[:, ks], Qh[:, qsl], Kl[:, ks],
                                          False, False))
                t_b = plan.emit("tensor", mk_mm(PS[:, ks], Ql[:, qsl],
                                                Kh[:, ks], False, True),
                                inc=s_pe)
                pm_last = vred(nm4[:, kb4:kb4 + 1], PS[:, ks], Alu.max,
                               waits=[t_b])
            t_nm = vred(nmax[:, :], nm4[:, 0:4], Alu.max, negate=True,
                        waits=[pm_last])
            t_exp = None
            for kb4 in range(4):
                ks = slice(kb4 * 512, (kb4 + 1) * 512)
                t_exp = act(A_sb[qb % 2][:, ks], PS[:, ks], Act.Exp,
                            bias=nmax[:, 0:1], scale=1.0,
                            accum=asum4[:, kb4:kb4 + 1],
                            waits=W(t_nm, A_free[qb % 2] if kb4 == 0 else None))
                ps_bank[kb4] = t_exp
                exp_bk[qb][kb4] = t_exp
            exp_toks[qb] = t_exp
            t_as = vred(asum[:, :], asum4[:, 0:4], Alu.add, waits=[t_exp])
            vrecip(rsums[:, sb0 + qb:sb0 + qb + 1], asum[:, 0:1], waits=[t_as])
            if qb >= 3 and qb % 2 == 1:
                av_pair((qb - 3) // 2)
            if qb >= 1:
                a_transposes(qb - 1)
            if quant and qb in (2, 5, 8, 11, 14):
                quant_chunks([3 + (2, 5, 8, 11, 14).index(qb)])
                if qb == 14:
                    xrel_free[0] = kst_free[0]
                    xrel_free[1] = kst_free[1]
        a_transposes(15)
        av_pair(7)
        state["ps_free"] = ps_bank[3]
        state["A_free2"] = A_free
        state["AT_free"] = AT_free
        state["hT_ready"] = hT_last

        gparts = [[None] * 4 for _ in range(NQT)]
        gxa = [None] * NQT
        ps_half = [state["ps_free"], state["ps_free"]]

        def emit_gln_pre(qt, half):
            for tb4 in (2 * half, 2 * half + 1):
                tb = qt * 4 + tb4
                i = tb % 2
                r = row0 + tb * 128
                t_x = load(xrel[i], x_hbm[r:r + 128, :],
                           waits=W(xrel_free[i]))
                gparts[qt][tb4] = xn_pre(tb, xrel[i], t_x, "vector",
                                         stats_tok[qt], stb=sb0 + tb)
                xrel_free[i] = gparts[qt][tb4]

        def emit_gln_tps(qt, half):
            j = qt % 2
            for tb4 in (2 * half, 2 * half + 1):
                tb = qt * 4 + tb4
                gparts[qt][tb4] = xn_tps(tb, j, gparts[qt][tb4],
                                         PF[:, :],
                                         W(pf_free[0], pf_free[1]))
                pf_free[0] = gparts[qt][tb4][1]
                pf_free[1] = gparts[qt][tb4][1]

        def emit_xa(qt):
            j = qt % 2
            qcols = slice(qt * 512, (qt + 1) * 512)
            evs = [t for pair in gparts[qt] for t in pair]
            specs = []
            for c in range(8):
                specs.append((PA[0:16, 512:1024], a_h[:, c * 16:(c + 1) * 16],
                              xfh[j][:, c * 512:(c + 1) * 512], c == 0, False))
                specs.append((PA[0:16, 512:1024], a_h[:, c * 16:(c + 1) * 16],
                              xfl[j][:, c * 512:(c + 1) * 512], False, False))
            for c in range(8):
                specs.append((PA[0:16, 512:1024], a_l[:, c * 16:(c + 1) * 16],
                              xfh[j][:, c * 512:(c + 1) * 512], False, c == 7))
            t_xa = mm_group(specs, waits=W(*evs, pa_free[1], t_ah, t_alo))
            t_xah = scopy(xa_h[:, qcols], PA[0:16, 512:1024], waits=[t_xa])
            t_xal = tt("vector", xa_l[:, qcols], PA[0:16, 512:1024],
                       xa_h[:, qcols], Alu.subtract, waits=[t_xah])
            pa_free[1] = t_xal
            gxa[qt] = (t_xah, t_xal)

        def emit_ffn_chunks(qt, cs):
            j = qt % 2
            qcols = slice(qt * 512, (qt + 1) * 512)
            t_xah, t_xal = gxa[qt]
            for c in cs:
                half = c % 2
                csl = slice(c * 512, (c + 1) * 512)
                ps = PA[:, half * 512:(half + 1) * 512]
                specs = []
                for kc in range(8):
                    st = q_sb[:, kc * D + c * 128: kc * D + (c + 1) * 128]
                    specs.append((ps, st,
                                  xfh[j][:, kc * 512:(kc + 1) * 512],
                                  kc == 0, False))
                    specs.append((ps, st,
                                  xfl[j][:, kc * 512:(kc + 1) * 512],
                                  False, False))
                bsl = slice(c * 128, (c + 1) * 128)
                specs.append((ps, bs_h[:, bsl], xa_h[:, qcols], False, False))
                specs.append((ps, bs_h[:, bsl], xa_l[:, qcols], False, False))
                specs.append((ps, bs_l[:, bsl], xa_h[:, qcols], False, False))
                specs.append((ps, bias2[:, bsl], ones2[:, 0:512],
                              False, True))
                t_y = mm_group(specs, waits=W(pa_free[half], t_xah, t_xal,
                                              qtoks["t_qdone"],
                                              qtoks["t_bsh"], qtoks["t_bsl"],
                                              qtoks["t_bh"], qtoks["t_d2"],
                                              t_one2))
                t_g = act(g_sb[:, csl], ps, Act.Silu, scale=invsc[:, 1:2],
                          waits=W(t_y, state["g_free"]))
                pa_free[half] = t_g
                gparts[qt][0 if c < 4 else 1] = t_y
                gsilu[qt][c] = t_g

        def emit_wo_sg(qt):
            qcols = slice(qt * 512, (qt + 1) * 512)
            for c in range(8):
                half = c % 2
                csl = slice(c * 512, (c + 1) * 512)
                wsl = slice(c * 128, (c + 1) * 128)
                ps = PA[:, half * 512:(half + 1) * 512]
                specs = [
                    (ps, wo_h[:, wsl], hTh[:, qcols], True, False),
                    (ps, wo_h[:, wsl], hTl[:, qcols], False, False),
                    (ps, wo_l[:, wsl], hTh[:, qcols], False, True),
                ]
                t_s = mm_group(specs, waits=W(pa_free[half],
                                              state["hT_ready"],
                                              wsp[3] if c == 0 else None))
                t_sg = tt("vector", g_sb[:, csl], ps, g_sb[:, csl], Alu.mult,
                          waits=W(t_s, gsilu[qt][c]))
                pa_free[half] = t_sg
                gsilu[qt][c] = t_sg

        def emit_sgtp_stores(qt):
            j = qt % 2
            last_tp = None
            for tb4 in range(4):
                tb = qt * 4 + tb4
                i = tb % 2
                hp = tb4 % 2
                r = row0 + tb * 128
                t_xr = load(xrel[i], x_hbm[r:r + 128, :],
                            waits=W(xrel_free[i]))
                sp = [(PS[:, hp * 1024 + c * 128: hp * 1024 + (c + 1) * 128],
                       g_sb[:, c * 512 + tb4 * 128: c * 512 + tb4 * 128 + 128],
                       identf[:, :], c % 4 == 0, c % 4 == 3, True)
                      for c in range(8)]
                t_tp = mm_group(sp, waits=W(gsilu[qt][7], ps_half[hp]))
                last_tp = t_tp
                t_out = stt("vector", outb[i],
                            PS[:, hp * 1024:(hp + 1) * 1024],
                            rsums[:, sb0 + tb:sb0 + tb + 1], xrel[i],
                            Alu.mult, Alu.add,
                            waits=W(t_tp, t_xr, outb_free[i],
                                    xbuf_free[i]))
                ps_half[hp] = t_out
                xrel_free[i] = t_out
                t_st = store(y_hbm[r:r + 128, :], outb[i],
                             waits=[t_out])
                outb_free[i] = t_st
                xbuf_free[i] = t_st
                total_stores[0] += 1
            state["g_free"] = last_tp
            xf_free[j] = (gparts[qt][0], gparts[qt][1])

        def emit_prefetch_next():
            row0n = (s + 1) * S
            sb0n = (s + 1) * NTB
            srcs = [xbuf[2], xbuf[3], Vt[:, 0:1024], Vt[:, 1024:2048]]
            afree1 = state["A_free2"][1]
            lds = []
            for tb4 in range(4):
                r = row0n + tb4 * 128
                w = W(xbuf_free[SLOT[tb4]]) if tb4 < 2 else W(state["AT_free"])
                lds.append(load(srcs[tb4], x_hbm[r:r + 128, :], waits=w))
            sq_last = None
            sums = []
            for tb4 in range(4):
                col = slice(sb0n + tb4, sb0n + tb4 + 1)
                t_sum = vred(ssum[:, col], srcs[tb4], Alu.add,
                             waits=[lds[tb4]])
                t_sq = act(xn_tm[2 + tb4 % 2][:, :], srcs[tb4], Act.Square,
                           accum=ssq[:, col],
                           waits=W(lds[tb4], xntm_free[2 + tb4 % 2],
                                   afree1, sq_last))
                sq_last = t_sq
                sums.append((t_sum, t_sq))
            c0 = sb0n
            t1 = ts1("vector", smean[:, c0:c0 + 4], ssum[:, c0:c0 + 4],
                     1.0 / D, Alu.mult, waits=[sums[-1][0], sums[-1][1]])
            t2 = ts1("vector", svar[:, c0:c0 + 4], ssq[:, c0:c0 + 4],
                     1.0 / D, Alu.mult, waits=[t1])
            t3 = tt("vector", stmp[:, 0:4], smean[:, c0:c0 + 4],
                    smean[:, c0:c0 + 4], Alu.mult, waits=[t2])
            t4 = tt("vector", svar[:, c0:c0 + 4], svar[:, c0:c0 + 4],
                    stmp[:, 0:4], Alu.subtract, waits=[t3])
            t5 = act(svar[:, c0:c0 + 4], svar[:, c0:c0 + 4], Act.Ln,
                     bias=epsb[:, 0:1], waits=[t4, t_eps])
            t6 = act(srstd[:, c0:c0 + 4], svar[:, c0:c0 + 4], Act.Exp,
                     scale=-0.5, waits=[t5])
            t7 = stt("vector", snmr[:, c0:c0 + 4], smean[:, c0:c0 + 4], -1.0,
                     srstd[:, c0:c0 + 4], Alu.mult, Alu.mult, waits=[t6])
            ln01 = []
            for tb4 in range(2):
                t_xn = ts2("vector", xn_tm[2 + tb4][:, :], srcs[tb4],
                           smean[:, sb0n + tb4:sb0n + tb4 + 1],
                           srstd[:, sb0n + tb4:sb0n + tb4 + 1],
                           Alu.subtract, Alu.mult,
                           waits=W(t7, lds[tb4], sq_last, afree1))
                ln01.append(t_xn)
            state["pre1"] = {"stats": t7, "ln01": ln01,
                             "src23": srcs[2:], "ld23": lds[2:]}

        gsilu = [[None] * 8 for _ in range(NQT)]
        for it in range(NQT + 1):
            if it < NQT:
                emit_gln_pre(it, 0)
            if it > 0:
                emit_ffn_chunks(it - 1, [0, 1, 2, 3])
            if it < NQT:
                emit_gln_tps(it, 0)
                emit_gln_pre(it, 1)
            if it > 0:
                emit_ffn_chunks(it - 1, [4, 5, 6, 7])
            if it < NQT:
                emit_gln_tps(it, 1)
            if it > 0:
                emit_wo_sg(it - 1)
            if it < NQT:
                emit_xa(it)
            if it > 0:
                emit_sgtp_stores(it - 1)
            if it == NQT - 1 and s + 1 < SEQ_PER_CORE:
                emit_prefetch_next()
        state["ps_free"] = ps_half[1]
        state["ps_free_h"] = (ps_half[0], ps_half[1])

    for s in range(SEQ_PER_CORE):
        seq_program(s, quant=(s == 0))

    def final_wait(e):
        last = None
        for sem in s_st:
            if sem.n > 0:
                last = e.wait_ge(sem.h, sem.n)
        return last
    plan.emit("scalar", final_wait, (), None)

    with ctx:
        for sem in sems:
            sem.h = ctx.enter_context(nc.semaphore(sem.name))

        def runner(stream):
            def run(e):
                last_wait = {}
                for waits, build, inc, amt in stream:
                    for sem, v in waits:
                        if v is None or v <= 0:
                            continue
                        if last_wait.get(sem.name, -1) >= v:
                            continue
                        e.wait_ge(sem.h, v)
                        last_wait[sem.name] = v
                    inst = build(e)
                    if inc is not None:
                        inst.then_inc(inc.h, amt)
            return run

        with nc.Block() as block:
            block.sync(runner(plan.streams["sync"]))
            block.tensor(runner(plan.streams["tensor"]))
            block.vector(runner(plan.streams["vector"]))
            block.scalar(runner(plan.streams["scalar"]))
            block.gpsimd(runner(plan.streams["gpsimd"]))
    return nc


_NC_CACHE = None


def kernel(L0_attn_x, L0_attn_head_dim_mask, L0_attn_w_q, L0_attn_w_k,
           L0_attn_w_v, L0_attn_w_o, L0_ffn_kernel, L0_ffn_bias, L0_ffn_a,
           L0_ffn_b):
    global _NC_CACHE
    x = np.ascontiguousarray(np.asarray(L0_attn_x, dtype=np.float32))
    mask = np.ascontiguousarray(
        np.asarray(L0_attn_head_dim_mask, dtype=np.int32).reshape(128, 1))
    wq = np.ascontiguousarray(np.asarray(L0_attn_w_q, dtype=np.float32))
    wk = np.ascontiguousarray(np.asarray(L0_attn_w_k, dtype=np.float32))
    wv = np.ascontiguousarray(np.asarray(L0_attn_w_v, dtype=np.float32))
    wo = np.ascontiguousarray(np.asarray(L0_attn_w_o, dtype=np.float32))
    kern = np.ascontiguousarray(np.asarray(L0_ffn_kernel, dtype=np.float32))
    fbias = np.ascontiguousarray(
        np.asarray(L0_ffn_bias, dtype=np.float32).reshape(1, D))
    la = np.ascontiguousarray(np.asarray(L0_ffn_a, dtype=np.float32))
    lb = np.ascontiguousarray(np.asarray(L0_ffn_b, dtype=np.float32))

    if _NC_CACHE is None:
        _NC_CACHE = build_nc()
    nc = _NC_CACHE

    in_maps = []
    for c in range(NCORES):
        xs = np.ascontiguousarray(
            x[c * SEQ_PER_CORE:(c + 1) * SEQ_PER_CORE].reshape(
                SEQ_PER_CORE * S, D))
        in_maps.append({
            "x": xs, "mask": mask, "wq": wq, "wk": wk, "wv": wv, "wo": wo,
            "kern": kern, "fbias": fbias, "la": la, "lb": lb,
        })
    res = run_bass_kernel_spmd(nc, in_maps, core_ids=list(range(NCORES)))
    out = np.concatenate([np.asarray(r["y"]).reshape(SEQ_PER_CORE, S, D)
                          for r in res.results], axis=0)
    return out.astype(np.float32)
